# version 30
# speedup vs baseline: 1.2275x; 1.0211x over previous
import numpy as np
from contextlib import ExitStack

import concourse.bass as bass
import concourse.mybir as mybir
from concourse.bass_utils import run_bass_kernel_spmd

F32 = mybir.dt.float32
BF16 = mybir.dt.bfloat16
AF = mybir.ActivationFunctionType
ALU = mybir.AluOpType
AX = mybir.AxisListType

D = 1024
SEQ = 4096
DT = 8
NBLK = 8
T = 512
NCH = 4
LN_EPS = 1e-5
EPS = 1e-6


class SemC:
    __slots__ = ("h", "v")

    def __init__(self, h):
        self.h = h
        self.v = 0

    def inc(self, instr, n=1):
        instr.then_inc(self.h, n)
        self.v += n
        return self.v


_GUARD_OPS = {
    "tensor_tensor": (("out",), ("in0", "in1")),
    "tensor_scalar": (("out", "accum_out"), ("in0", "scalar1", "scalar2")),
    "scalar_tensor_tensor": (("out", "accum_out"), ("in0", "scalar", "in1")),
    "tensor_copy": (("out",), ("in_",)),
    "tensor_reduce": (("out",), ("in_",)),
    "reduce_sum": (("out",), ("in_",)),
    "reduce_max": (("out",), ("in_",)),
    "memset": (("ap",), ()),
    "reciprocal": (("out",), ("in_",)),
    "iota": (("out",), ()),
    "activation": (("out", "accum_out"), ("in_", "bias", "scale")),
    "copy": (("out",), ("in_",)),
    "square": (("out",), ("in_",)),
    "add": (("out",), ("in_",)),
    "mul": (("out",), ("in_",)),
}


class _Guard:

    def __init__(self, eng, state):
        self._e = eng
        self._s = state

    @staticmethod
    def _name(ap):
        try:
            return ap.tensor.name
        except AttributeError:
            return None

    def __getattr__(self, op):
        eng = self._e
        fn = getattr(eng, op)
        sig = _GUARD_OPS.get(op)
        if sig is None:
            return fn
        outs, ins = sig
        w = self._s["w"]
        r = self._s["r"]

        def call(*args, **kw):
            in_names = []
            out_names = []
            for key in ins:
                n = self._name(kw.get(key))
                if n is not None:
                    in_names.append(n)
            for key in outs:
                n = self._name(kw.get(key))
                if n is not None:
                    out_names.append(n)
            need = (any(n in w for n in in_names)
                    or any(n in w for n in out_names)
                    or any(n in r for n in out_names))
            if need:
                eng.drain()
                w.clear()
                r.clear()
            res = fn(*args, **kw)
            w.update(out_names)
            r.update(in_names)
            return res

        return call


def build_program(dump=None):
    nc = bass.Bass("TRN2", target_bir_lowering=False,
                   dynamic_dma_scratch_size=256)

    hs = nc.dram_tensor("hs", [SEQ, D], F32, kind="ExternalInput").ap()
    wqT = nc.dram_tensor("wqT", [D, D], F32, kind="ExternalInput").ap()
    wkT = nc.dram_tensor("wkT", [D, D], F32, kind="ExternalInput").ap()
    wvT = nc.dram_tensor("wvT", [D, D], F32, kind="ExternalInput").ap()
    woT = nc.dram_tensor("woT", [D, D], F32, kind="ExternalInput").ap()
    memw = nc.dram_tensor("memw", [D, D], F32, kind="ExternalInput").ap()
    mn = nc.dram_tensor("mn", [D], F32, kind="ExternalInput").ap()
    gate = nc.dram_tensor("gate", [1], F32, kind="ExternalInput").ap()
    cw1 = nc.dram_tensor("cw1", [D, D], F32, kind="ExternalInput").ap()
    cb1 = nc.dram_tensor("cb1", [D], F32, kind="ExternalInput").ap()
    cw2 = nc.dram_tensor("cw2", [D, 512], F32, kind="ExternalInput").ap()
    cb2 = nc.dram_tensor("cb2", [512], F32, kind="ExternalInput").ap()
    cw3 = nc.dram_tensor("cw3", [512, 256], F32, kind="ExternalInput").ap()
    cb3 = nc.dram_tensor("cb3", [256], F32, kind="ExternalInput").ap()
    cw4 = nc.dram_tensor("cw4", [256, 1], F32, kind="ExternalInput").ap()
    cb4 = nc.dram_tensor("cb4", [1], F32, kind="ExternalInput").ap()
    gw1 = nc.dram_tensor("gw1", [D, 512], F32, kind="ExternalInput").ap()
    gb1 = nc.dram_tensor("gb1", [512], F32, kind="ExternalInput").ap()
    gw2 = nc.dram_tensor("gw2", [512, 256], F32, kind="ExternalInput").ap()
    gb2 = nc.dram_tensor("gb2", [256], F32, kind="ExternalInput").ap()
    gw3 = nc.dram_tensor("gw3", [256, 1], F32, kind="ExternalInput").ap()
    gb3 = nc.dram_tensor("gb3", [1], F32, kind="ExternalInput").ap()
    ar = nc.dram_tensor("ar", [1], F32, kind="ExternalInput").ap()

    out = nc.dram_tensor("out", [SEQ, D], F32, kind="ExternalOutput").ap()

    x0f = nc.dram_tensor("x0f", [DT, 128, SEQ], F32, kind="Internal").ap()
    x1f = nc.dram_tensor("x1f", [DT, 128, SEQ], F32, kind="Internal").ap()
    x2f = nc.dram_tensor("x2f", [DT, 128, SEQ], F32, kind="Internal").ap()

    dbg = None
    if dump is not None:
        dbg = nc.dram_tensor("dbg", [DT, 128, SEQ], F32, kind="ExternalOutput").ap()

    st = ExitStack()
    _gw = {"v": {"w": set(), "r": set()}, "s": {"w": set(), "r": set()},
           "g": {"w": set(), "r": set()}}

    class _RecInstr:
        __slots__ = ("post",)

        def __init__(self, post):
            self.post = post

        def then_inc(self, sem, n=1):
            self.post.append((sem, n))
            return self

    class _RecEng:

        def __init__(self, q):
            self._q = q

        def __getattr__(self, name):
            q = self._q

            def call(*a, **k):
                post = []
                q.append((name, a, k, post))
                return _RecInstr(post)

            return call

    class Deferred:

        def __init__(self, blk):
            self._blk = blk
            self._q = {k: [] for k in ("sync", "tensor", "scalar", "vector",
                                       "gpsimd")}

        def sync(self, f):
            f(_RecEng(self._q["sync"]))

        def tensor(self, f):
            f(_RecEng(self._q["tensor"]))

        def scalar(self, f):
            f(_RecEng(self._q["scalar"]))

        def vector(self, f):
            f(_RecEng(self._q["vector"]))

        def gpsimd(self, f):
            f(_RecEng(self._q["gpsimd"]))

        def flush(self):
            guards = {"scalar": "s", "vector": "v", "gpsimd": "g"}
            for k in ("sync", "tensor", "scalar", "vector", "gpsimd"):
                ops = self._q[k]
                if not ops:
                    continue
                gk = guards.get(k)

                def body(e, ops=ops, gk=gk):
                    if gk is not None:
                        e = _Guard(e, _gw[gk])
                    for name, a, kw, post in ops:
                        i = getattr(e, name)(*a, **kw)
                        for sem, n in post:
                            i.then_inc(sem, n)

                getattr(self._blk, k)(body)

    from contextlib import contextmanager

    def GV(blk):
        return blk.vector

    def GS(blk):
        return blk.scalar

    def GG(blk):
        return blk.gpsimd

    @contextmanager
    def DBlk():
        with nc.Block() as real:
            d = Deferred(real)
            yield d
            d.flush()

    with st:
        E = st.enter_context

        idf32 = E(nc.sbuf_tensor("idf32", [128, 128], F32))
        idbf = E(nc.sbuf_tensor("idbf", [128, 128], BF16))
        ones_col_bf = E(nc.sbuf_tensor("ones_col_bf", [128, 1], BF16))
        ones_row_bf = E(nc.sbuf_tensor("ones_row_bf", [1, 128], BF16))
        ones_row_f32 = E(nc.sbuf_tensor("ones_row_f32", [1, 128], F32))
        mask_tpl = E(nc.sbuf_tensor("mask_tpl", [128, 512], BF16))
        x_sb = E(nc.sbuf_tensor("x_sb", [128, 2, DT, T], F32))
        xn_bf = E(nc.sbuf_tensor("xn_bf", [128, DT, T], BF16))
        xb_bf = E(nc.sbuf_tensor("xb_bf", [128, DT, T], BF16))
        xsq_bf = E(nc.sbuf_tensor("xsq_bf", [128, DT, T], BF16))
        a_sb = E(nc.sbuf_tensor("a_sb", [128, T], F32))
        b_sb = E(nc.sbuf_tensor("b_sb", [128, T], F32))
        row1 = E(nc.sbuf_tensor("row1", [1, T], F32))
        row2 = E(nc.sbuf_tensor("row2", [1, T], F32))
        row3 = E(nc.sbuf_tensor("row3", [1, T], F32))
        row4 = E(nc.sbuf_tensor("row4", [1, T], F32))
        g_s = E(nc.sbuf_tensor("g_s", [1, 8], F32))
        c0_1 = E(nc.sbuf_tensor("c0_1", [1, 1], F32))
        c1_1 = E(nc.sbuf_tensor("c1_1", [1, 1], F32))
        c0_2 = E(nc.sbuf_tensor("c0_2", [1, 1], F32))
        c1_2 = E(nc.sbuf_tensor("c1_2", [1, 1], F32))
        b4h = E(nc.sbuf_tensor("b4h", [1, 1], F32))
        gb3r = E(nc.sbuf_tensor("gb3r", [1, 1], F32))
        ar_r = E(nc.sbuf_tensor("ar_r", [1, 1], F32))
        b1c = E(nc.sbuf_tensor("b1c", [128, 8], F32))
        b2c = E(nc.sbuf_tensor("b2c", [128, 4], F32))
        b3c = E(nc.sbuf_tensor("b3c", [128, 2], F32))
        gb1c = E(nc.sbuf_tensor("gb1c", [128, 4], F32))
        gb2c = E(nc.sbuf_tensor("gb2c", [128, 2], F32))
        mnc_bf = E(nc.sbuf_tensor("mnc_bf", [128, 8], BF16))
        m_bf = E(nc.sbuf_tensor("m_bf", [128, 8], BF16))
        h1g = E(nc.sbuf_tensor("h1g", [128, 4], BF16))
        h2g = E(nc.sbuf_tensor("h2g", [128, 2], BF16))

        def mksem(name):
            return SemC(E(nc.semaphore(name)))

        s_boot = mksem("s_boot")
        s_wst2 = [mksem("p0_wst0"), mksem("p0_wst1"), mksem("p0_wst2")]

        with (nc.psum_tensor("pinit", [128, 2, 256], F32) as pinit,
              nc.sbuf_tensor("iota_a", [128, 128], F32) as iota_a,
              nc.sbuf_tensor("iota_b", [128, 128], F32) as iota_b,
              nc.sbuf_tensor("bstage", [32, 128], F32) as bstage,
              nc.sbuf_tensor("mn_row", [1, D], F32) as mn_row):
            with DBlk() as blk:
                @blk.sync
                def _(q):
                    q.dma_start(out=bstage[0:8, 0:128], in_=cb1.rearrange("(a b) -> a b", b=128)).then_inc(s_boot.h, 16)
                    q.dma_start(out=bstage[8:12, 0:128], in_=cb2.rearrange("(a b) -> a b", b=128)).then_inc(s_boot.h, 16)
                    q.dma_start(out=bstage[12:14, 0:128], in_=cb3.rearrange("(a b) -> a b", b=128)).then_inc(s_boot.h, 16)
                    q.dma_start(out=bstage[14:18, 0:128], in_=gb1.rearrange("(a b) -> a b", b=128)).then_inc(s_boot.h, 16)
                    q.dma_start(out=bstage[18:20, 0:128], in_=gb2.rearrange("(a b) -> a b", b=128)).then_inc(s_boot.h, 16)
                    q.dma_start(out=bstage[20:28, 0:128], in_=mn.rearrange("(a b) -> a b", b=128)).then_inc(s_boot.h, 16)
                    q.dma_start(out=row1[0:1, 0:1], in_=gate[None, :]).then_inc(s_boot.h, 16)
                    q.dma_start(out=row1[0:1, 1:2], in_=cb4[None, :]).then_inc(s_boot.h, 16)
                    q.dma_start(out=row1[0:1, 2:3], in_=gb3[None, :]).then_inc(s_boot.h, 16)
                    q.dma_start(out=row1[0:1, 3:4], in_=ar[None, :]).then_inc(s_boot.h, 16)
                s_boot.v += 10 * 16
            with DBlk() as blk:
                @blk.sync
                def _(q):
                    q.dma_start(out=mn_row[0:1, 0:D], in_=mn[None, :]).then_inc(s_boot.h, 16)
                s_boot.v += 16

                @GG(blk)
                def _(g):
                    g.iota(iota_a[:, :], [[1, 128]], channel_multiplier=0,
                           allow_small_or_imprecise_dtypes=True)
                    g.iota(iota_b[:, :], [[0, 128]], channel_multiplier=1,
                           allow_small_or_imprecise_dtypes=True)
                    g.memset(ones_col_bf[:, :], 1.0)
                    g.memset(ones_row_bf[:, :], 1.0)
                    g.memset(ones_row_f32[:, :], 1.0)
                    g.memset(mask_tpl[:, 128:512], 1.0)

            with DBlk() as blk:
                @GV(blk)
                def _(v):
                    v.tensor_tensor(out=mask_tpl[:, 0:128], in0=iota_a[:, :],
                                    in1=iota_b[:, :], op=ALU.is_ge)

            with DBlk() as blk:
                @GV(blk)
                def _(v):
                    v.wait_ge(s_boot.h, s_boot.v)
                    v.tensor_tensor(out=idbf[:, :], in0=iota_a[:, :],
                                    in1=iota_b[:, :], op=ALU.is_equal)
                    v.tensor_tensor(out=idf32[:, :], in0=iota_a[:, :],
                                    in1=iota_b[:, :], op=ALU.is_equal)

            with DBlk() as blk:
                @blk.tensor
                def _(te):
                    te.transpose(out=pinit[:, 0, 0:28], in_=bstage[0:28, 0:128],
                                 identity=idf32[0:28, 0:28])

            with DBlk() as blk:
                @GV(blk)
                def _(v):
                    v.tensor_copy(out=b1c[:, :], in_=pinit[:, 0, 0:8])
                    v.tensor_copy(out=b2c[:, :], in_=pinit[:, 0, 8:12])
                    v.tensor_copy(out=b3c[:, :], in_=pinit[:, 0, 12:14])
                    v.tensor_copy(out=gb1c[:, :], in_=pinit[:, 0, 14:18])
                    v.tensor_copy(out=gb2c[:, :], in_=pinit[:, 0, 18:20])
                    v.tensor_copy(out=mnc_bf[:, :], in_=pinit[:, 0, 20:28])
                    v.tensor_scalar(out=b4h[:, :], in0=row1[0:1, 1:2],
                                    scalar1=0.5, scalar2=None, op0=ALU.mult)
                    v.tensor_copy(out=gb3r[:, :], in_=row1[0:1, 2:3])
                    v.tensor_copy(out=ar_r[:, :], in_=row1[0:1, 3:4])
                    v.reduce_sum(out=g_s[0:1, 6:7], in_=mn_row[0:1, 0:D], axis=AX.X)

            with DBlk() as blk:
                @GS(blk)
                def _(s):
                    s.activation(out=g_s[0:1, 0:1], in_=row1[0:1, 0:1], func=AF.Sigmoid)
                    s.activation(out=g_s[0:1, 1:2], in_=row1[0:1, 0:1], func=AF.Sigmoid,
                                 scale=-1.0)

            with DBlk() as blk:
                @GV(blk)
                def _(v):
                    v.tensor_scalar(out=g_s[0:1, 4:5], in0=g_s[0:1, 6:7],
                                    scalar1=EPS, scalar2=None, op0=ALU.is_ge)
                    v.tensor_tensor(out=g_s[0:1, 7:8], in0=g_s[0:1, 0:1],
                                    in1=g_s[0:1, 4:5], op=ALU.mult)
                    v.tensor_scalar(out=g_s[0:1, 7:8], in0=g_s[0:1, 7:8],
                                    scalar1=1e-38, scalar2=None, op0=ALU.max)

            with DBlk() as blk:
                @GS(blk)
                def _(s):
                    s.activation(out=g_s[0:1, 2:3], in_=g_s[0:1, 1:2], func=AF.Ln)
                    s.activation(out=g_s[0:1, 3:4], in_=g_s[0:1, 7:8], func=AF.Ln)

        ln1mg = g_s[0:1, 2:3]
        lnga = g_s[0:1, 3:4]


        wslot_free = {}
        wslot_cnt = {}

        def emit_weight_load(blk, dram_ap, sb_tile, nkt, M, wstage, s_wld2, s_wcast,
                             use_scalar=False):
            for kt in range(nkt):
                for c0 in range(0, M, 512):
                    cw = min(512, M - c0)
                    n = wslot_cnt.get(id(wstage), 0)
                    wslot_cnt[id(wstage)] = n + 1
                    slot = n % 2
                    sw = s_wld2[slot]
                    free_at = wslot_free.get((id(wstage), slot))

                    @blk.sync
                    def _(q, kt=kt, c0=c0, cw=cw, slot=slot, free_at=free_at,
                          sw=sw):
                        if free_at is not None:
                            q.wait_ge(s_wcast.h, free_at)
                        q.dma_start(out=wstage[:, slot, 0:cw],
                                    in_=dram_ap[kt * 128:(kt + 1) * 128,
                                                c0:c0 + cw])\
                            .then_inc(sw.h, 16)
                    sw.v += 16

                    eng = GS(blk) if use_scalar else GV(blk)

                    @eng
                    def _(e, kt=kt, c0=c0, cw=cw, slot=slot, sw=sw, ld=sw.v):
                        e.wait_ge(sw.h, ld)
                        if use_scalar:
                            i = e.copy(out=sb_tile[:, kt, c0:c0 + cw],
                                       in_=wstage[:, slot, 0:cw])
                        else:
                            i = e.tensor_copy(out=sb_tile[:, kt, c0:c0 + cw],
                                              in_=wstage[:, slot, 0:cw])
                        i.then_inc(s_wcast.h, 1)
                    s_wcast.v += 1
                    wslot_free[(id(wstage), slot)] = s_wcast.v

        def emit_ln(blk, x3, pC, pAB, pR, sems, want_out=None, te_pre=None):
            s_te, s_ve, s_sc = sems

            @GS(blk)
            def _(s):
                s.activation(out=xsq_bf[:, :, :], in_=x3, func=AF.Square)\
                    .then_inc(s_sc.h, 1)
            s_sc.v += 1
            sq_done = s_sc.v

            @GV(blk)
            def _(v):
                v.tensor_copy(out=xb_bf[:, :, :], in_=x3).then_inc(s_ve.h, 1)
            s_ve.v += 1
            xb_done = s_ve.v

            @blk.tensor
            def _(te):
                if te_pre is not None:
                    te_pre(te)
                te.wait_ge(s_ve.h, xb_done)
                te.wait_ge(s_sc.h, sq_done)
                for dt in range(DT):
                    te.matmul(out=pR[0:1, 0:T], lhsT=ones_col_bf[:, :],
                              rhs=xb_bf[:, dt, :], start=(dt == 0), stop=(dt == DT - 1))
                for dt in range(DT):
                    i = te.matmul(out=pR[0:1, 512:512 + T], lhsT=ones_col_bf[:, :],
                                  rhs=xsq_bf[:, dt, :], start=(dt == 0), stop=(dt == DT - 1))
                i.then_inc(s_te.h, 1)
            s_te.v += 1
            sums_done = s_te.v

            srow = pR[0:1, 0:T]
            qrow = pR[0:1, 512:512 + T]

            @GV(blk)
            def _(v):
                v.wait_ge(s_te.h, sums_done)
                v.scalar_tensor_tensor(out=row1[:, :], in0=srow, scalar=1.0 / 1024,
                                       in1=srow, op0=ALU.mult, op1=ALU.mult)
                v.tensor_tensor(out=row2[:, :], in0=qrow, in1=row1[:, :], op=ALU.subtract)
                v.tensor_scalar(out=row2[:, :], in0=row2[:, :], scalar1=1.0 / 1024,
                                scalar2=LN_EPS, op0=ALU.mult, op1=ALU.add)\
                    .then_inc(s_ve.h, 1)
            s_ve.v += 1
            var_done = s_ve.v

            @GS(blk)
            def _(s):
                s.wait_ge(s_ve.h, var_done)
                s.activation(out=row3[:, :], in_=row2[:, :], func=AF.Ln)
                s.activation(out=row2[:, :], in_=row3[:, :], func=AF.Exp, scale=-0.5)\
                    .then_inc(s_sc.h, 1)
            s_sc.v += 1
            rstd_done = s_sc.v

            @GV(blk)
            def _(v):
                v.wait_ge(s_sc.h, rstd_done)
                v.scalar_tensor_tensor(out=row3[:, :], in0=srow, scalar=-1.0 / 1024,
                                       in1=row2[:, :], op0=ALU.mult, op1=ALU.mult)\
                    .then_inc(s_ve.h, 1)
            s_ve.v += 1
            rows_done = s_ve.v

            @blk.tensor
            def _(te):
                te.wait_ge(s_ve.h, rows_done)
                te.matmul(out=pAB[:, 0, :], lhsT=ones_row_f32[:, :], rhs=row2[:, :],
                          start=True, stop=True)
                i = te.matmul(out=pAB[:, 1, :], lhsT=ones_row_f32[:, :], rhs=row3[:, :],
                              start=True, stop=True)
                i.then_inc(s_te.h, 1)
            s_te.v += 1
            ab_done = s_te.v

            @GV(blk)
            def _(v):
                v.wait_ge(s_te.h, ab_done)
                v.tensor_copy(out=a_sb[:, :], in_=pAB[:, 0, :])
                v.tensor_copy(out=b_sb[:, :], in_=pAB[:, 1, :])
                for dt in range(DT):
                    v.tensor_tensor(out=pC[:, dt % 2, :], in0=x3[:, dt, :],
                                    in1=a_sb[:, :], op=ALU.mult)
                    if want_out is not None:
                        v.tensor_tensor(out=want_out[:, dt, :], in0=pC[:, dt % 2, :],
                                        in1=b_sb[:, :], op=ALU.add)
                    i = v.tensor_tensor(out=xn_bf[:, dt, :], in0=pC[:, dt % 2, :],
                                        in1=b_sb[:, :], op=ALU.add)
                i.then_inc(s_ve.h, 1)
            s_ve.v += 1
            return s_ve.v

        def emit_gfnet(blk, gw1_sb, gw2_sb, gw3_sb, pG, c0_t, c1_t, sems,
                       m_ready=None):
            s_te, s_ve, s_sc = sems

            @blk.tensor
            def _(te):
                if m_ready is not None:
                    te.wait_ge(m_ready[0].h, m_ready[1])
                for et in range(4):
                    for dt in range(DT):
                        i = te.matmul(out=pG[:, 0, et:et + 1],
                                      lhsT=gw1_sb[:, dt, et * 128:(et + 1) * 128],
                                      rhs=m_bf[:, dt:dt + 1],
                                      start=(dt == 0), stop=(dt == DT - 1),
                                      skip_group_check=True)
                i.then_inc(s_te.h, 1)
            s_te.v += 1
            l1_done = s_te.v

            @GS(blk)
            def _(s):
                s.wait_ge(s_te.h, l1_done)
                for et in range(4):
                    i = s.activation(out=h1g[:, et:et + 1], in_=pG[:, 0, et:et + 1],
                                     func=AF.Gelu, bias=gb1c[:, et:et + 1])
                i.then_inc(s_sc.h, 1)
            s_sc.v += 1
            h1_done = s_sc.v

            @blk.tensor
            def _(te):
                te.wait_ge(s_sc.h, h1_done)
                for et in range(2):
                    for kt in range(4):
                        i = te.matmul(out=pG[:, 1, et:et + 1],
                                      lhsT=gw2_sb[:, kt, et * 128:(et + 1) * 128],
                                      rhs=h1g[:, kt:kt + 1],
                                      start=(kt == 0), stop=(kt == 3),
                                      skip_group_check=True)
                i.then_inc(s_te.h, 1)
            s_te.v += 1
            l2_done = s_te.v

            @GS(blk)
            def _(s):
                s.wait_ge(s_te.h, l2_done)
                for et in range(2):
                    i = s.activation(out=h2g[:, et:et + 1], in_=pG[:, 1, et:et + 1],
                                     func=AF.Gelu, bias=gb2c[:, et:et + 1])
                i.then_inc(s_sc.h, 1)
            s_sc.v += 1
            h2_done = s_sc.v

            @blk.tensor
            def _(te):
                te.wait_ge(s_sc.h, h2_done)
                for kt in range(2):
                    i = te.matmul(out=pG[0:1, 0, 8:9], lhsT=gw3_sb[:, kt, 0:1],
                                  rhs=h2g[:, kt:kt + 1], start=(kt == 0), stop=(kt == 1),
                                  skip_group_check=True)
                i.then_inc(s_te.h, 1)
            s_te.v += 1
            l3_done = s_te.v

            @GS(blk)
            def _(s):
                s.wait_ge(s_te.h, l3_done)
                s.activation(out=g_s[0:1, 5:6], in_=pG[0:1, 0, 8:9], func=AF.Sigmoid,
                             bias=gb3r[:, :]).then_inc(s_sc.h, 1)
            s_sc.v += 1
            gf_done = s_sc.v

            @GV(blk)
            def _(v):
                v.wait_ge(s_sc.h, gf_done)
                v.tensor_scalar(out=c0_t[:, :], in0=g_s[0:1, 5:6], scalar1=1.4,
                                scalar2=-0.7, op0=ALU.mult, op1=ALU.add)
                v.tensor_tensor(out=c0_t[:, :], in0=c0_t[:, :], in1=ar_r[:, :],
                                op=ALU.mult)
                v.tensor_scalar(out=c0_t[:, :], in0=c0_t[:, :], scalar1=0.1,
                                scalar2=None, op0=ALU.add)
                v.tensor_scalar(out=c1_t[:, :], in0=ar_r[:, :], scalar1=0.3,
                                scalar2=None, op0=ALU.mult)

        class Slot:
            __slots__ = ("free",)

            def __init__(self):
                self.free = None

            def wait(self, eng):
                if self.free is not None:
                    eng.wait_ge(self.free[0].h, self.free[1])

            def set(self, semc, val=None):
                self.free = (semc, semc.v if val is None else val)

        awq = nc.dram_tensor("awq", [D, D], BF16, kind="Internal").ap()
        awk = nc.dram_tensor("awk", [D, D], BF16, kind="Internal").ap()
        awv = nc.dram_tensor("awv", [D, D], BF16, kind="Internal").ap()
        awm = nc.dram_tensor("awm", [D, D], BF16, kind="Internal").ap()
        awo = nc.dram_tensor("awo", [D, D], BF16, kind="Internal").ap()

        with ExitStack() as p0s, nc.named_scope("P0"):
            E0 = p0s.enter_context
            x0t = E0(nc.sbuf_tensor("x0t", [128, 2, 4, D], F32))
            x0t_bf = E0(nc.sbuf_tensor("x0t_bf", [128, 2, 4, D], BF16))
            xq0_bf = E0(nc.sbuf_tensor("xq0_bf", [128, 2, 4, D], BF16))
            stg = E0(nc.sbuf_tensor("stg", [128, 2, DT, T], F32))
            sumc = E0(nc.sbuf_tensor("sumc", [128, 4], F32))
            sqc = E0(nc.sbuf_tensor("sqc", [128, 4], F32))
            tc1 = E0(nc.sbuf_tensor("tc1", [128, 4], F32))
            tc2 = E0(nc.sbuf_tensor("tc2", [128, 4], F32))
            acolb = E0(nc.sbuf_tensor("acolb", [128, 2, 4], BF16))
            bcol = E0(nc.sbuf_tensor("bcol", [128, 4], F32))
            bacc = E0(nc.sbuf_tensor("bacc", [128, 1], F32))
            bacc_bf = E0(nc.sbuf_tensor("bacc_bf", [128, 1], BF16))
            gw1_sb = E0(nc.sbuf_tensor("gw1_sb0", [128, DT, 512], BF16))
            gw2_sb = E0(nc.sbuf_tensor("gw2_sb0", [128, 4, 256], BF16))
            gw3_sb = E0(nc.sbuf_tensor("gw3_sb0", [128, 2, 1], BF16))
            wstage = E0(nc.sbuf_tensor("wstage0", [128, 2, D], F32))
            wstage_a = E0(nc.sbuf_tensor("wstage0a", [128, 6, D], F32))
            wcast_bf = E0(nc.sbuf_tensor("wcast_bf0", [128, 3, D], BF16))
            mrow = E0(nc.sbuf_tensor("mrow", [1, D], F32))
            mrow_bf = E0(nc.sbuf_tensor("mrow_bf", [1, D], BF16))

            pm = E0(nc.psum_tensor("pm", [1, D], F32))
            ptr = E0(nc.psum_tensor("ptr", [128, 2, 512], F32))
            pG = E0(nc.psum_tensor("pG0", [128, 2, 16], F32))
            pmt = E0(nc.psum_tensor("pmt", [128, 16], BF16))

            s_ld2 = [mksem("p0_ld0"), mksem("p0_ld1")]
            s_st2 = [mksem("p0_st0"), mksem("p0_st1")]
            s_te = mksem("p0_te")
            s_ve = mksem("p0_ve")
            s_sc = mksem("p0_sc")
            s_wld2 = [mksem("p0_wld0"), mksem("p0_wld1")]
            s_wlda2 = [mksem("p0_wlda%d" % i) for i in range(6)]
            s_wcast = mksem("p0_wcast")
            s_wcast_a = mksem("p0_wcast_a")

            xslot = [Slot(), Slot()]
            stgslot = [Slot(), Slot()]
            ptrslot = [Slot(), Slot()]
            x0bfslot = [Slot(), Slot()]
            xq0slot = [Slot(), Slot()]
            acolslot = [Slot(), Slot()]

            with DBlk() as blk:
                @GG(blk)
                def _(g):
                    g.memset(bacc[:, :], 0.0)

            with DBlk() as blk:
                emit_weight_load(blk, gw1, gw1_sb, DT, 512, wstage, s_wld2, s_wcast)
                emit_weight_load(blk, gw2, gw2_sb, 4, 256, wstage, s_wld2, s_wcast)
                emit_weight_load(blk, gw3, gw3_sb, 2, 1, wstage, s_wld2, s_wcast)

                strip_list = [(wsrc, wdst, kt)
                              for wsrc, wdst in [(wqT, awq), (wkT, awk), (wvT, awv),
                                                 (memw, awm), (woT, awo)]
                              for kt in range(DT)]
                strip_pos = [0]

                def emit_weight_strips(n):
                    while strip_pos[0] < len(strip_list) and n > 0:
                        nstrip = strip_pos[0]
                        strip_pos[0] += 1
                        n -= 1
                        wsrc, wdst, kt = strip_list[nstrip]
                        if True:
                            slot = nstrip % 6
                            bslot = nstrip % 3
                            sw = s_wlda2[slot]
                            sst = s_wst2[bslot]
                            free_at = wslot_free.get((id(wstage_a), slot))

                            @GS(blk)
                            def _(s, kt=kt, slot=slot, free_at=free_at, wsrc=wsrc,
                                  sw=sw):
                                if free_at is not None:
                                    s.wait_ge(s_wcast_a.h, free_at)
                                s.dma_start(out=wstage_a[:, slot, :],
                                            in_=wsrc[kt * 128:(kt + 1) * 128, :])\
                                    .then_inc(sw.h, 16)
                            sw.v += 16

                            @GS(blk)
                            def _(s, kt=kt, slot=slot, bslot=bslot, sw=sw, ld=sw.v,
                                  sst=sst, st=sst.v):
                                s.wait_ge(sw.h, ld)
                                if st >= 16:
                                    s.wait_ge(sst.h, st)
                                s.copy(out=wcast_bf[:, bslot, :],
                                       in_=wstage_a[:, slot, :])\
                                    .then_inc(s_wcast_a.h, 1)
                            s_wcast_a.v += 1
                            wslot_free[(id(wstage_a), slot)] = s_wcast_a.v

                            @GS(blk)
                            def _(s, kt=kt, bslot=bslot, wc=s_wcast_a.v, wdst=wdst,
                                  sst=sst):
                                s.wait_ge(s_wcast_a.h, wc)
                                s.dma_start(out=wdst[kt * 128:(kt + 1) * 128, :],
                                            in_=wcast_bf[:, bslot, :]).then_inc(sst.h, 16)
                            sst.v += 16

                for b in range(NBLK):
                    slot = b % 2

                    s_ld = s_ld2[slot]

                    @blk.sync
                    def _(q, b=b, slot=slot, s_ld=s_ld):
                        xslot[slot].wait(q)
                        q.dma_start(
                            out=x0t[:, slot, :, :],
                            in_=hs[b * T:(b + 1) * T, :].rearrange("(tt p) d -> p tt d", p=128),
                        ).then_inc(s_ld.h, 16)
                    s_ld.v += 16
                    ld_done = s_ld.v

                    @GS(blk)
                    def _(s, b=b, slot=slot, s_ld=s_ld, ld=ld_done):
                        s.wait_ge(s_ld.h, ld)
                        xq0slot[b % 2].wait(s)
                        s.activation(out=xq0_bf[:, b % 2, :, :], in_=x0t[:, slot, :, :],
                                     func=AF.Square).then_inc(s_sc.h, 1)
                    s_sc.v += 1
                    sq_done = s_sc.v

                    @GV(blk)
                    def _(v, b=b, slot=slot, s_ld=s_ld, ld=ld_done, sq=sq_done):
                        v.wait_ge(s_ld.h, ld)
                        x0bfslot[b % 2].wait(v)
                        v.tensor_copy(out=x0t_bf[:, b % 2, :, :], in_=x0t[:, slot, :, :])
                        v.reduce_sum(out=sumc[:, :], in_=x0t[:, slot, :, :], axis=AX.X)
                        v.wait_ge(s_sc.h, sq)
                        v.reduce_sum(out=sqc[:, :], in_=xq0_bf[:, b % 2, :, :], axis=AX.X)
                        v.scalar_tensor_tensor(out=tc1[:, :], in0=sumc[:, :],
                                               scalar=1.0 / (1024.0 * 1024.0),
                                               in1=sumc[:, :], op0=ALU.mult, op1=ALU.mult)
                        v.tensor_scalar(out=tc2[:, :], in0=sqc[:, :], scalar1=1.0 / 1024,
                                        scalar2=None, op0=ALU.mult)
                        v.tensor_tensor(out=tc2[:, :], in0=tc2[:, :], in1=tc1[:, :],
                                        op=ALU.subtract)
                        v.tensor_scalar(out=tc2[:, :], in0=tc2[:, :], scalar1=LN_EPS,
                                        scalar2=None, op0=ALU.add).then_inc(s_ve.h, 1)
                    s_ve.v += 1
                    var_done = s_ve.v

                    @GS(blk)
                    def _(s, b=b, var=var_done):
                        s.wait_ge(s_ve.h, var)
                        s.activation(out=tc1[:, :], in_=tc2[:, :], func=AF.Ln)
                        acolslot[b % 2].wait(s)
                        s.activation(out=acolb[:, b % 2, :], in_=tc1[:, :], func=AF.Exp,
                                     scale=-0.5).then_inc(s_sc.h, 1)
                    s_sc.v += 1
                    rstd_done = s_sc.v

                    @GV(blk)
                    def _(v, b=b, rs=rstd_done):
                        v.wait_ge(s_sc.h, rs)
                        v.scalar_tensor_tensor(out=bcol[:, :], in0=sumc[:, :],
                                               scalar=-1.0 / 1024,
                                               in1=acolb[:, b % 2, :],
                                               op0=ALU.mult, op1=ALU.mult)
                        v.tensor_reduce(out=tc1[:, 0:1], in_=bcol[:, :], axis=AX.X,
                                        op=ALU.add)
                        v.tensor_tensor(out=bacc[:, :], in0=bacc[:, :], in1=tc1[:, 0:1],
                                        op=ALU.add).then_inc(s_ve.h, 1)
                    s_ve.v += 1
                    cols_done = s_ve.v

                    for dt in range(DT):
                        pslot = dt % 2

                        @blk.tensor
                        def _(te, slot=slot, dt=dt, pslot=pslot, ld=ld_done, s_ld=s_ld):
                            if dt == 0:
                                te.wait_ge(s_ld.h, ld)
                            ptrslot[pslot].wait(te)
                            for tt in range(4):
                                i = te.transpose(
                                    out=ptr[:, pslot, tt * 128:(tt + 1) * 128],
                                    in_=x0t[:, slot, tt, dt * 128:(dt + 1) * 128],
                                    identity=idf32[:, :])
                            i.then_inc(s_te.h, 1)
                        s_te.v += 1
                        tr_done = s_te.v

                        @GV(blk)
                        def _(v, b=b, dt=dt, pslot=pslot, td=tr_done):
                            v.wait_ge(s_te.h, td)
                            if dt == 0:
                                stgslot[b % 2].wait(v)
                            v.tensor_copy(out=stg[:, b % 2, dt, :],
                                          in_=ptr[:, pslot, :]).then_inc(s_ve.h, 1)
                        s_ve.v += 1
                        ptrslot[pslot].set(s_ve)

                    @blk.tensor
                    def _(te, b=b, cd=cols_done):
                        te.wait_ge(s_ve.h, cd)
                        for tt in range(4):
                            for eh in range(2):
                                i = te.matmul(out=pm[0:1, eh * 512:(eh + 1) * 512],
                                              lhsT=acolb[:, b % 2, tt:tt + 1],
                                              rhs=x0t_bf[:, b % 2, tt, eh * 512:(eh + 1) * 512],
                                              start=(b == 0 and tt == 0),
                                              stop=(b == NBLK - 1 and tt == 3))
                        i.then_inc(s_te.h, 1)
                    s_te.v += 1
                    x0bfslot[b % 2].set(s_te)
                    xq0slot[b % 2].set(s_ve)
                    acolslot[b % 2].set(s_te)

                    xslot[slot].set(s_te)

                    s_st = s_st2[b % 2]

                    @blk.sync
                    def _(q, b=b, ev=s_ve.v, s_st=s_st):
                        q.wait_ge(s_ve.h, ev)
                        q.dma_start(
                            out=x0f[:, :, b * T:(b + 1) * T].rearrange("dt p t -> p dt t"),
                            in_=stg[:, b % 2, :, :],
                        ).then_inc(s_st.h, 16)
                    s_st.v += 16
                    stgslot[b % 2].set(s_st)
                    if b >= 1:
                        emit_weight_strips(6)

                @GV(blk)
                def _(v, te_v=s_te.v):
                    v.wait_ge(s_te.h, te_v)
                    v.tensor_copy(out=mrow[0:1, :], in_=pm[0:1, :])
                    v.tensor_copy(out=bacc_bf[:, :], in_=bacc[:, :]).then_inc(s_ve.h, 1)
                s_ve.v += 1
                ev0 = s_ve.v

                @blk.tensor
                def _(te, ev0=ev0):
                    te.wait_ge(s_ve.h, ev0)
                    te.matmul(out=pG[0:1, 0, 8:9], lhsT=bacc_bf[:, :],
                              rhs=ones_col_bf[:, :], start=True, stop=True,
                              skip_group_check=True).then_inc(s_te.h, 1)
                s_te.v += 1
                bs_done = s_te.v

                @GV(blk)
                def _(v, bs=bs_done):
                    v.wait_ge(s_te.h, bs)
                    v.tensor_copy(out=row1[0:1, 0:1], in_=pG[0:1, 0, 8:9])
                    v.tensor_scalar(out=mrow[0:1, :], in0=mrow[0:1, :],
                                    scalar1=row1[0:1, 0:1], scalar2=1.0 / SEQ,
                                    op0=ALU.add, op1=ALU.mult).then_inc(s_ve.h, 1)
                s_ve.v += 1
                m_done = s_ve.v

                @GS(blk)
                def _(s, m=m_done):
                    s.wait_ge(s_ve.h, m)
                    s.copy(out=mrow_bf[0:1, :], in_=mrow[0:1, :]).then_inc(s_sc.h, 1)
                s_sc.v += 1
                mb_done = s_sc.v

                @blk.tensor
                def _(te, mb=mb_done):
                    te.wait_ge(s_sc.h, mb)
                    for dt in range(DT):
                        i = te.transpose(out=pmt[:, 2 * dt:2 * dt + 1],
                                         in_=mrow_bf[0:1, dt * 128:(dt + 1) * 128],
                                         identity=ones_row_bf[0:1, 0:1])
                    i.then_inc(s_te.h, 1)
                s_te.v += 1
                mt_done = s_te.v

                @GV(blk)
                def _(v, mt=mt_done, wc=s_wcast.v):
                    v.wait_ge(s_te.h, mt)
                    v.wait_ge(s_wcast.h, wc)
                    v.tensor_copy(out=m_bf[:, :], in_=pmt[:, 0:16:2]).then_inc(s_ve.h, 1)
                s_ve.v += 1

                emit_weight_strips(99)
                emit_gfnet(blk, gw1_sb, gw2_sb, gw3_sb, pG, c0_1, c1_1,
                           (s_te, s_ve, s_sc), m_ready=(s_ve, s_ve.v))

                @blk.sync
                def _(q, vals=[(s.h, s.v) for s in s_st2 + s_wst2]):
                    for h, v_ in vals:
                        q.wait_ge(h, v_)

        def emit_ffn_pass(tag, xin, xout, c0_t, c1_t, final_output):
            with ExitStack() as ps, nc.named_scope(tag):
                Ep = ps.enter_context
                cw1_sb = Ep(nc.sbuf_tensor(tag + "cw1", [128, DT, D], BF16))
                cw2_sb = Ep(nc.sbuf_tensor(tag + "cw2", [128, DT, 512], BF16))
                cw3_sb = Ep(nc.sbuf_tensor(tag + "cw3", [128, 4, 256], BF16))
                cw4_sb = Ep(nc.sbuf_tensor(tag + "cw4", [128, 2, 1], BF16))
                h1_bf = Ep(nc.sbuf_tensor(tag + "h1", [128, 2, DT, T], BF16))
                h2_bf = Ep(nc.sbuf_tensor(tag + "h2", [128, 2, 4, T], BF16))
                h3_bf = Ep(nc.sbuf_tensor(tag + "h3", [128, 2, T], BF16))
                out_f = Ep(nc.sbuf_tensor(tag + "outf", [128, 2, DT, T], F32))
                xnp = Ep(nc.sbuf_tensor(tag + "xnp", [128, 2, DT, T], BF16))
                x_sb2 = Ep(nc.sbuf_tensor(tag + "xsb2", [128, 2, DT, T], F32))
                crow_bf = Ep(nc.sbuf_tensor(tag + "crow", [1, 2, T], BF16))
                twt0 = Ep(nc.sbuf_tensor(tag + "twt0", [128, T], F32))
                twt1 = Ep(nc.sbuf_tensor(tag + "twt1", [128, T], F32))
                wstage = Ep(nc.sbuf_tensor(tag + "wst", [128, 2, 512], F32))
                if final_output:
                    outtok = Ep(nc.sbuf_tensor(tag + "otok", [128, 2, D], F32))

                pH = Ep(nc.psum_tensor(tag + "pH", [128, 2, T], F32))
                pG_ = Ep(nc.psum_tensor(tag + "pG", [128, 2, T], F32))
                pAB = Ep(nc.psum_tensor(tag + "pAB", [128, 2, T], F32))
                pR = Ep(nc.psum_tensor(tag + "pR", [128, 2, T], F32))

                s_ld2 = [mksem(tag + "ld0"), mksem(tag + "ld1"),
                         mksem(tag + "ld2"), mksem(tag + "ld3")]
                s_st2 = [mksem(tag + "st0"), mksem(tag + "st1"),
                         mksem(tag + "st2"), mksem(tag + "st3")]
                s_te = mksem(tag + "te")
                s_ve = mksem(tag + "ve")
                s_sc = mksem(tag + "sc")
                s_po = mksem(tag + "po")
                s_wld2 = [mksem(tag + "wld0"), mksem(tag + "wld1")]
                s_wcast = mksem(tag + "wcast")

                xslot = [Slot(), Slot(), Slot(), Slot()]
                xslot_sc = [Slot(), Slot(), Slot(), Slot()]

                def xbuf(b):
                    buf = x_sb if (b // 2) % 2 == 0 else x_sb2
                    return buf[:, b % 2]
                pHs = [Slot(), Slot()]
                pGs = [Slot(), Slot()]
                pABs = Slot()
                pRs = Slot()
                xnslot = Slot()
                xbslot = Slot()
                otoks = [Slot(), Slot()]
                outf_slot = [Slot(), Slot()]
                st8 = {}

                with DBlk() as blk:
                    emit_weight_load(blk, cw1, cw1_sb, DT, D, wstage, s_wld2, s_wcast)
                    w1_ready = s_wcast.v
                    emit_weight_load(blk, cw2, cw2_sb, DT, 512, wstage, s_wld2, s_wcast)
                    w2_ready = s_wcast.v
                    emit_weight_load(blk, cw3, cw3_sb, 4, 256, wstage, s_wld2, s_wcast)
                    emit_weight_load(blk, cw4, cw4_sb, 2, 1, wstage, s_wld2, s_wcast)
                    w4_ready = s_wcast.v

                    def emit_ld(b):
                        slot = b % 4
                        s_ld = s_ld2[b % 4]

                        @blk.sync
                        def _(q, b=b, slot=slot, s_ld=s_ld):
                            xslot[slot].wait(q)
                            xslot_sc[slot].wait(q)
                            q.dma_start(
                                out=xbuf(b),
                                in_=xin[:, :, b * T:(b + 1) * T].rearrange("dt p t -> p dt t"),
                            ).then_inc(s_ld.h, 16)
                        s_ld.v += 16
                        st8[(b, "ld")] = s_ld.v

                    def emit_ln_stats(b):
                        slot = b % 4
                        s_ld = s_ld2[b % 4]
                        ld_done = st8[(b, "ld")]

                        @blk.scalar
                        def _(s, b=b, slot=slot, s_ld=s_ld, ld=ld_done):
                            s.wait_ge(s_ld.h, ld)
                            xnslot.wait(s)
                            s.activation(out=xsq_bf[:, :, :], in_=xbuf(b),
                                         func=AF.Square).then_inc(s_sc.h, 1)
                        s_sc.v += 1
                        sq_done = s_sc.v
                        xslot_sc[slot].set(s_sc)

                        @blk.vector
                        def _(v, b=b, slot=slot, s_ld=s_ld, ld=ld_done):
                            v.wait_ge(s_ld.h, ld)
                            xbslot.wait(v)
                            v.tensor_copy(out=xb_bf[:, :, :], in_=xbuf(b))\
                                .then_inc(s_ve.h, 1)
                        s_ve.v += 1
                        xb_done = s_ve.v

                        @blk.tensor
                        def _(te, sq=sq_done, xb=xb_done):
                            te.wait_ge(s_sc.h, sq)
                            te.wait_ge(s_ve.h, xb)
                            pRs.wait(te)
                            for dt in range(DT):
                                te.matmul(out=pR[0:1, 0, :], lhsT=ones_col_bf[:, :],
                                          rhs=xb_bf[:, dt, :], start=(dt == 0),
                                          stop=(dt == DT - 1))
                            for dt in range(DT):
                                i = te.matmul(out=pR[0:1, 1, :], lhsT=ones_col_bf[:, :],
                                              rhs=xsq_bf[:, dt, :], start=(dt == 0),
                                              stop=(dt == DT - 1))
                            i.then_inc(s_te.h, 1)
                        s_te.v += 1
                        sums_done = s_te.v
                        xnslot.set(s_te)
                        xbslot.set(s_te)

                        srow = pR[0:1, 0, :]
                        qrow = pR[0:1, 1, :]

                        @blk.vector
                        def _(v, sd=sums_done):
                            v.wait_ge(s_te.h, sd)
                            v.tensor_copy(out=row1[:, :], in_=srow)
                            v.scalar_tensor_tensor(out=row2[:, :], in0=row1[:, :],
                                                   scalar=1.0 / 1024, in1=row1[:, :],
                                                   op0=ALU.mult, op1=ALU.mult)
                            v.tensor_tensor(out=row2[:, :], in0=qrow, in1=row2[:, :],
                                            op=ALU.subtract)
                            v.tensor_scalar(out=row2[:, :], in0=row2[:, :],
                                            scalar1=1.0 / 1024, scalar2=LN_EPS,
                                            op0=ALU.mult, op1=ALU.add).then_inc(s_ve.h, 1)
                        s_ve.v += 1
                        var_done = s_ve.v

                        @blk.scalar
                        def _(s, var=var_done):
                            s.wait_ge(s_ve.h, var)
                            s.activation(out=row3[:, :], in_=row2[:, :], func=AF.Ln)
                            s.activation(out=row2[:, :], in_=row3[:, :], func=AF.Exp,
                                         scale=-0.5).then_inc(s_sc.h, 1)
                        s_sc.v += 1
                        rstd_done = s_sc.v

                        @blk.vector
                        def _(v, rs=rstd_done):
                            v.wait_ge(s_sc.h, rs)
                            v.scalar_tensor_tensor(out=row3[:, :], in0=row1[:, :],
                                                   scalar=-1.0 / 1024,
                                                   in1=row2[:, :], op0=ALU.mult,
                                                   op1=ALU.mult).then_inc(s_ve.h, 1)
                        s_ve.v += 1
                        st8[(b, "rows")] = s_ve.v

                    def emit_stats_rep(b):
                        rows_done = st8[(b, "rows")]

                        @blk.tensor
                        def _(te, rd=rows_done):
                            te.wait_ge(s_ve.h, rd)
                            te.matmul(out=pR[:, 0, :], lhsT=ones_row_f32[:, :],
                                      rhs=row2[:, :], start=True, stop=True)
                            te.matmul(out=pR[:, 1, :], lhsT=ones_row_f32[:, :],
                                      rhs=row3[:, :], start=True, stop=True)\
                                .then_inc(s_te.h, 1)
                        s_te.v += 1
                        ab_done = s_te.v
                        st8[(b, "ab")] = ab_done

                    def emit_ln_affine(b):
                        slot = b % 4
                        ab_done = st8[(b, "ab")]

                        @blk.vector
                        def _(v, slot=slot, b=b, ab=ab_done):
                            v.wait_ge(s_te.h, ab)
                            outf_slot[b % 2].wait(v)
                            pABs.wait(v)
                            v.tensor_copy(out=b_sb[:, :], in_=pR[:, 1, :])
                            tq = [pAB[:, 1, :], twt0[:, :], twt1[:, :],
                                  pAB[:, 1, :]]
                            for half in range(2):
                                def do_m(j):
                                    dt = half * 4 + j
                                    v.tensor_tensor(out=tq[j],
                                                    in0=xbuf(b)[:, dt, :],
                                                    in1=pR[:, 0, :], op=ALU.mult)

                                def do_a(j):
                                    dt = half * 4 + j
                                    return v.tensor_tensor(
                                        out=out_f[:, b % 2, dt, :],
                                        in0=tq[j], in1=b_sb[:, :], op=ALU.add)
                                do_m(0); do_m(1); do_m(2); do_a(0); do_m(3)
                                do_a(1); do_a(2); i = do_a(3)
                            i.then_inc(s_ve.h, 1)
                        s_ve.v += 1
                        aff_done = s_ve.v
                        pRs.set(s_ve)

                        @blk.scalar
                        def _(s, b=b, af=aff_done):
                            s.wait_ge(s_ve.h, af)
                            s.copy(out=xnp[:, b % 2, :, :], in_=out_f[:, b % 2, :, :])\
                                .then_inc(s_sc.h, 1)
                        s_sc.v += 1
                        st8[(b, "rdy")] = ("sc", s_sc.v)

                    def emit_ln(b):
                        emit_ln_stats(b)
                        emit_stats_rep(b)
                        emit_ln_affine(b)

                    def emit_cycle(b, cyc):
                        eng, rdy = st8[(b, "rdy")]
                        rdy_sem = {"sc": s_sc, "ve": s_ve}[eng]
                        for et in range(DT):
                            psl = et % 2

                            @blk.tensor
                            def _(te, b=b, et=et, psl=psl, rdy=rdy, rdy_sem=rdy_sem,
                                  w1=w1_ready):
                                if et == 0:
                                    te.wait_ge(rdy_sem.h, rdy)
                                    te.wait_ge(s_wcast.h, w1)
                                pHs[psl].wait(te)
                                for dt in range(DT):
                                    i = te.matmul(
                                        out=pH[:, psl, :],
                                        lhsT=cw1_sb[:, dt, et * 128:(et + 1) * 128],
                                        rhs=xnp[:, b % 2, dt, :],
                                        start=(dt == 0), stop=(dt == DT - 1))
                                i.then_inc(s_te.h, 1)
                            s_te.v += 1
                            mm_done = s_te.v

                            @blk.scalar
                            def _(s, b=b, et=et, psl=psl, mm=mm_done):
                                s.wait_ge(s_te.h, mm)
                                s.activation(out=h1_bf[:, b % 2, et, :],
                                             in_=pH[:, psl, :],
                                             func=AF.Gelu, bias=b1c[:, et:et + 1])\
                                    .then_inc(s_sc.h, 1)
                            s_sc.v += 1
                            pHs[psl].set(s_sc)
                        h1_done = s_sc.v

                        for et in range(4):
                            psl = et % 2

                            @blk.tensor
                            def _(te, b=b, et=et, psl=psl, h1=h1_done, w2=w2_ready):
                                if et == 0:
                                    te.wait_ge(s_sc.h, h1)
                                    te.wait_ge(s_wcast.h, w2)
                                pGs[psl].wait(te)
                                for kt in range(DT):
                                    i = te.matmul(
                                        out=pG_[:, psl, :],
                                        lhsT=cw2_sb[:, kt, et * 128:(et + 1) * 128],
                                        rhs=h1_bf[:, b % 2, kt, :],
                                        start=(kt == 0), stop=(kt == DT - 1))
                                i.then_inc(s_te.h, 1)
                            s_te.v += 1
                            mm_done = s_te.v

                            @blk.scalar
                            def _(s, b=b, et=et, psl=psl, mm=mm_done):
                                s.wait_ge(s_te.h, mm)
                                s.activation(out=h2_bf[:, b % 2, et, :],
                                             in_=pG_[:, psl, :],
                                             func=AF.Gelu, bias=b2c[:, et:et + 1])\
                                    .then_inc(s_sc.h, 1)
                            s_sc.v += 1
                            pGs[psl].set(s_sc)
                        h2_done = s_sc.v

                        for et in range(2):
                            psl = et % 2

                            @blk.tensor
                            def _(te, b=b, et=et, psl=psl, h2=h2_done):
                                if et == 0:
                                    te.wait_ge(s_sc.h, h2)
                                pHs[psl].wait(te)
                                for kt in range(4):
                                    i = te.matmul(
                                        out=pH[:, psl, :],
                                        lhsT=cw3_sb[:, kt, et * 128:(et + 1) * 128],
                                        rhs=h2_bf[:, b % 2, kt, :],
                                        start=(kt == 0), stop=(kt == 3))
                                i.then_inc(s_te.h, 1)
                            s_te.v += 1
                            mm_done = s_te.v

                            @blk.scalar
                            def _(s, b=b, et=et, psl=psl, mm=mm_done):
                                s.wait_ge(s_te.h, mm)
                                s.activation(out=h3_bf[:, et, :],
                                             in_=pH[:, psl, :],
                                             func=AF.Gelu, bias=b3c[:, et:et + 1])\
                                    .then_inc(s_sc.h, 1)
                            s_sc.v += 1
                            pHs[psl].set(s_sc)
                        h3_done = s_sc.v

                        @blk.tensor
                        def _(te, b=b, h3=h3_done, w4=w4_ready):
                            te.wait_ge(s_sc.h, h3)
                            te.wait_ge(s_wcast.h, w4)
                            pGs[0].wait(te)
                            for kt in range(2):
                                i = te.matmul(out=pG_[0:1, 0, :],
                                              lhsT=cw4_sb[:, kt, 0:1],
                                              rhs=h3_bf[:, kt, :],
                                              start=(kt == 0), stop=(kt == 1))
                            i.then_inc(s_te.h, 1)
                        s_te.v += 1
                        t4_done = s_te.v

                        @blk.scalar
                        def _(s, t4=t4_done):
                            s.wait_ge(s_te.h, t4)
                            s.activation(out=row4[:, :], in_=pG_[0:1, 0, :],
                                         func=AF.Tanh, scale=0.5, bias=b4h[:, :])\
                                .then_inc(s_sc.h, 1)
                        s_sc.v += 1
                        ttf_done = s_sc.v
                        pGs[0].set(s_sc)

                        @blk.vector
                        def _(v, b=b, tf=ttf_done):
                            v.wait_ge(s_sc.h, tf)
                            v.tensor_scalar(out=crow_bf[:, b % 2, :], in0=row4[:, :],
                                            scalar1=c1_t[:, :], scalar2=c0_t[:, :],
                                            op0=ALU.mult, op1=ALU.add)\
                                .then_inc(s_ve.h, 1)
                        s_ve.v += 1
                        crow_done = s_ve.v

                        @blk.tensor
                        def _(te, b=b, cd=crow_done):
                            te.wait_ge(s_ve.h, cd)
                            pABs.wait(te)
                            te.matmul(out=pAB[:, 0, :], lhsT=ones_row_bf[:, :],
                                      rhs=crow_bf[:, b % 2, :], start=True, stop=True)\
                                .then_inc(s_te.h, 1)
                        s_te.v += 1
                        crep_done = s_te.v

                        @blk.vector
                        def _(v, b=b, cyc=cyc, cr=crep_done):
                            v.wait_ge(s_te.h, cr)
                            srcq = [2, 3, 0, 1]
                            sgn = [ALU.add, ALU.subtract, ALU.subtract, ALU.add]
                            tq = [pAB[:, 1, :], twt0[:, :], twt1[:, :],
                                  pAB[:, 1, :]]
                            for rnd in range(2):
                                def do_m(q_):
                                    dts = 2 * srcq[q_] + rnd
                                    v.tensor_tensor(out=tq[q_],
                                                    in0=xnp[:, b % 2, dts, :],
                                                    in1=pAB[:, 0, :], op=ALU.mult)

                                def do_a(q_):
                                    return v.tensor_tensor(
                                        out=out_f[:, b % 2, 2 * q_ + rnd, :],
                                        in0=out_f[:, b % 2, 2 * q_ + rnd, :],
                                        in1=tq[q_], op=sgn[q_])
                                do_m(0); do_m(1); do_m(2); do_a(0); do_m(3)
                                do_a(1); do_a(2); i = do_a(3)
                            i.then_inc(s_ve.h, 1)
                        s_ve.v += 1
                        twist_done = s_ve.v
                        pABs.set(s_ve)

                        st8[(b, "tw")] = twist_done
                        st8[(b, "rdy")] = ("ve", twist_done)

                    def emit_cast(b):
                        tw = st8[(b, "tw")]

                        @blk.scalar
                        def _(s, b=b, tw=tw):
                            s.wait_ge(s_ve.h, tw)
                            s.copy(out=xnp[:, b % 2, :, :],
                                   in_=out_f[:, b % 2, :, :]).then_inc(s_sc.h, 1)
                        s_sc.v += 1
                        st8[(b, "rdy")] = ("sc", s_sc.v)

                    def emit_res(b):
                        slot = b % 4
                        eng, rdy = st8[(b, "rdy")]
                        rdy_sem = {"sc": s_sc, "ve": s_ve}[eng]

                        @blk.vector
                        def _(v, b=b, slot=slot, rdy=rdy, rdy_sem=rdy_sem):
                            v.wait_ge(rdy_sem.h, rdy)
                            v.scalar_tensor_tensor(out=xbuf(b),
                                                   in0=out_f[:, b % 2, :, :],
                                                   scalar=0.5,
                                                   in1=xbuf(b),
                                                   op0=ALU.mult, op1=ALU.add)\
                                .then_inc(s_ve.h, 1)
                        s_ve.v += 1
                        res_done = s_ve.v
                        outf_slot[b % 2].set(s_ve)

                        if not final_output:
                            sst = s_st2[b % 4]

                            @blk.sync
                            def _(q, b=b, rd=res_done, sst=sst):
                                q.wait_ge(s_ve.h, rd)
                                q.dma_start(
                                    out=xout[:, :, b * T:(b + 1) * T].rearrange("dt p t -> p dt t"),
                                    in_=xbuf(b)).then_inc(sst.h, 16)
                            sst.v += 16
                            xslot[slot].set(sst)
                        else:
                            last_tr = None
                            for tt in range(4):
                                @blk.tensor
                                def _(te, b=b, tt=tt, rd=res_done):
                                    if tt == 0:
                                        te.wait_ge(s_ve.h, rd)
                                    pHs[0].wait(te)
                                    pHs[1].wait(te)
                                    for dt in range(DT):
                                        i = te.transpose(
                                            out=pH[:, dt // 4, (dt % 4) * 128:(dt % 4 + 1) * 128],
                                            in_=xbuf(b)[:, dt, tt * 128:(tt + 1) * 128],
                                            identity=idf32[:, :])
                                    i.then_inc(s_te.h, 1)
                                s_te.v += 1
                                tr_done = s_te.v

                                @blk.vector
                                def _(v, tt=tt, td=tr_done):
                                    v.wait_ge(s_te.h, td)
                                    otoks[tt % 2].wait(v)
                                    v.tensor_copy(out=outtok[:, tt % 2, 0:512],
                                                  in_=pH[:, 0, :])
                                    v.tensor_copy(out=outtok[:, tt % 2, 512:1024],
                                                  in_=pH[:, 1, :]).then_inc(s_ve.h, 1)
                                s_ve.v += 1
                                ev_done = s_ve.v
                                pHs[0].set(s_ve)
                                pHs[1].set(s_ve)
                                last_tr = s_te.v

                                sst = s_st2[tt % 2]

                                @blk.sync
                                def _(q, b=b, tt=tt, ev=ev_done, sst=sst):
                                    q.wait_ge(s_ve.h, ev)
                                    q.dma_start(
                                        out=out[b * T + tt * 128: b * T + (tt + 1) * 128, :],
                                        in_=outtok[:, tt % 2, :]).then_inc(sst.h, 16)
                                sst.v += 16
                                otoks[tt % 2].set(sst)
                            xslot[b % 4].set(s_te, last_tr)

                    emit_ld(0)
                    emit_ld(1)
                    emit_ln(0)
                    emit_ln(1)
                    for b in range(0, NBLK, 2):
                        if b + 2 < NBLK:
                            emit_ld(b + 2)
                            emit_ld(b + 3)
                        emit_cycle(b, 0)
                        if b > 0:
                            emit_stats_rep(b + 1)
                            emit_ln_affine(b + 1)
                        emit_cycle(b + 1, 0)
                        emit_cast(b)
                        emit_cast(b + 1)
                        if b + 2 < NBLK:
                            emit_ln_stats(b + 2)
                        emit_cycle(b, 1)
                        if b + 2 < NBLK:
                            emit_stats_rep(b + 2)
                        emit_cycle(b + 1, 1)
                        emit_cast(b)
                        emit_cast(b + 1)
                        emit_cycle(b, 2)
                        emit_cycle(b + 1, 2)
                        emit_res(b)
                        if b + 2 < NBLK:
                            emit_ln_affine(b + 2)
                        if b + 3 < NBLK:
                            emit_ln_stats(b + 3)
                        emit_res(b + 1)

                    @blk.sync
                    def _(q, vals=[(s.h, s.v) for s in s_st2]):
                        for h, v_ in vals:
                            q.wait_ge(h, v_)

        emit_ffn_pass("p1", x0f, x1f, c0_1, c1_1, False)

        with ExitStack() as ps, nc.named_scope("P2"):
            Ep = ps.enter_context
            wq_strip = Ep(nc.sbuf_tensor("wq_strip", [128, 2, DT, 128], BF16))
            wk_strip = Ep(nc.sbuf_tensor("wk_strip", [128, 2, DT, 128], BF16))
            wv_sb = Ep(nc.sbuf_tensor("wv_sb", [128, DT, D], BF16))
            wo_strip = Ep(nc.sbuf_tensor("wo_strip", [128, 2, DT, 128], BF16))
            S_f = Ep(nc.sbuf_tensor("S_f", [128, DT, D], F32))
            S_bfs = Ep(nc.sbuf_tensor("S_bfs", [128, 2, DT, 128], BF16))
            sq_bf = Ep(nc.sbuf_tensor("sq_bf", [128, 2, DT, T], BF16))
            sk_bf = Ep(nc.sbuf_tensor("sk_bf", [128, 2, DT, T], BF16))
            sk_tok = Ep(nc.sbuf_tensor("sk_tok", [128, NCH, D], BF16))
            v_bf = Ep(nc.sbuf_tensor("v_bf", [128, 2, NCH, D], BF16))
            A_bf = Ep(nc.sbuf_tensor("A_bf", [128, 2, 1280], BF16))
            xn_p2 = Ep(nc.sbuf_tensor("xn_p2", [128, 2, DT, T], BF16))
            t_rb = Ep(nc.sbuf_tensor("t_rb", [128, T], BF16))
            t_eb = Ep(nc.sbuf_tensor("t_eb", [128, T], BF16))
            rl_bc = Ep(nc.sbuf_tensor("rl_bc", [128, 2, T], BF16))
            rm_bc = Ep(nc.sbuf_tensor("rm_bc", [128, 2, T], BF16))
            tv1 = Ep(nc.sbuf_tensor("tv1", [128, 2, T], BF16))
            tv2 = Ep(nc.sbuf_tensor("tv2", [128, 2, T], BF16))
            zc = Ep(nc.sbuf_tensor("zc", [128, 8], F32))
            zc_bf = Ep(nc.sbuf_tensor("zc_bf", [128, 8], BF16))
            zred = Ep(nc.sbuf_tensor("zred", [128, 8], F32))

            pA_ = Ep(nc.psum_tensor("pA_", [128, 2, 512], F32))
            pB_ = Ep(nc.psum_tensor("pB_", [128, 2, 512], F32))
            pC_ = Ep(nc.psum_tensor("pC_", [128, 2, 512], F32))
            pD_ = Ep(nc.psum_tensor("pD_", [128, 2, 512], F32))

            s_ld2 = [mksem("a_ld0"), mksem("a_ld1")]
            s_st2 = [mksem("a_st0"), mksem("a_st1")]
            s_te = mksem("a_te")
            s_ve = mksem("a_ve")
            s_sc = mksem("a_sc")
            s_po = mksem("a_po")
            s_wld = mksem("a_wld")
            s_wq2 = [mksem("a_wq0"), mksem("a_wq1")]
            s_wk2 = [mksem("a_wk0"), mksem("a_wk1")]
            s_wm2 = [mksem("a_wm0"), mksem("a_wm1")]
            s_wo2 = [mksem("a_wo0"), mksem("a_wo1")]
            s_se = mksem("a_se")

            O_bf = xn_bf
            mem_strip = wq_strip
            AOFF = [0, 512, 896, 1152]

            xslot = [Slot(), Slot()]
            xslot_sc = [Slot(), Slot()]
            pAs = [Slot(), Slot()]
            pBs = [Slot(), Slot()]
            pCs = [Slot(), Slot()]
            pDs = [Slot(), Slot()]
            pRs = Slot()
            xnslot = Slot()
            xbslot = Slot()
            Sbf_slot = [Slot(), Slot()]
            wq_slot = [Slot(), Slot()]
            wm_slot = wq_slot
            wk_slot = [Slot(), Slot()]
            wo_slot = [Slot(), Slot()]
            trb_slot = [Slot(), Slot()]
            tvslot = [Slot(), Slot()]
            O_slot = Slot()
            sqbf = [Slot(), Slot()]
            skbf = [Slot(), Slot()]
            sktok_slot = Slot()
            vbf = [Slot(), Slot()]
            Abf = [Slot(), Slot()]
            st2 = {}

            with DBlk() as blk:
                @blk.sync
                def _(q, vals=[(s.h, s.v) for s in s_wst2]):
                    for h, v_ in vals:
                        q.wait_ge(h, v_)
                    q.dma_start(out=wv_sb[:, :, :],
                                in_=awv.rearrange("(dt p) m -> p dt m", p=128))\
                        .then_inc(s_wld.h, 16)
                s_wld.v += 16
                w_ready = s_wld.v
                supd_prev = 0
                zupd_prev = 0

                def emit_ldx(b):
                    slot = b % 2
                    s_ld = s_ld2[slot]

                    @blk.sync
                    def _(q, b=b, slot=slot, s_ld=s_ld):
                        xslot[slot].wait(q)
                        xslot_sc[slot].wait(q)
                        q.dma_start(
                            out=x_sb[:, slot, :, :],
                            in_=x1f[:, :, b * T:(b + 1) * T].rearrange("dt p t -> p dt t"),
                        ).then_inc(s_ld.h, 16)
                    s_ld.v += 16
                    st2[(b, "ld")] = s_ld.v

                def emit_lnq(b):
                    slot = b % 2
                    s_ld = s_ld2[slot]
                    ld_done = st2[(b, "ld")]

                    @blk.scalar
                    def _(s, slot=slot, s_ld=s_ld, ld=ld_done):
                        s.wait_ge(s_ld.h, ld)
                        xnslot.wait(s)
                        s.activation(out=xsq_bf[:, :, :], in_=x_sb[:, slot, :, :],
                                     func=AF.Square).then_inc(s_sc.h, 1)
                    s_sc.v += 1
                    sq_done = s_sc.v
                    xslot_sc[slot].set(s_sc)

                    @blk.vector
                    def _(v, slot=slot, s_ld=s_ld, ld=ld_done):
                        v.wait_ge(s_ld.h, ld)
                        xbslot.wait(v)
                        v.tensor_copy(out=xb_bf[:, :, :], in_=x_sb[:, slot, :, :])\
                            .then_inc(s_ve.h, 1)
                    s_ve.v += 1
                    xb_done = s_ve.v

                    @blk.tensor
                    def _(te, sq=sq_done, xb=xb_done):
                        te.wait_ge(s_sc.h, sq)
                        te.wait_ge(s_ve.h, xb)
                        pRs.wait(te)
                        pDs[0].wait(te)
                        pDs[1].wait(te)
                        for dt in range(DT):
                            te.matmul(out=pD_[0:1, 0, :], lhsT=ones_col_bf[:, :],
                                      rhs=xb_bf[:, dt, :], start=(dt == 0),
                                      stop=(dt == DT - 1))
                        for dt in range(DT):
                            i = te.matmul(out=pD_[0:1, 1, :], lhsT=ones_col_bf[:, :],
                                          rhs=xsq_bf[:, dt, :], start=(dt == 0),
                                          stop=(dt == DT - 1))
                        i.then_inc(s_te.h, 1)
                    s_te.v += 1
                    sums_done = s_te.v
                    xnslot.set(s_te)
                    xbslot.set(s_te)

                    srow = pD_[0:1, 0, :]
                    qrow = pD_[0:1, 1, :]

                    @blk.vector
                    def _(v, sd=sums_done):
                        v.wait_ge(s_te.h, sd)
                        v.tensor_copy(out=row1[:, :], in_=srow)
                        v.scalar_tensor_tensor(out=row2[:, :], in0=row1[:, :],
                                               scalar=1.0 / 1024, in1=row1[:, :],
                                               op0=ALU.mult, op1=ALU.mult)
                        v.tensor_tensor(out=row2[:, :], in0=qrow, in1=row2[:, :],
                                        op=ALU.subtract)
                        v.tensor_scalar(out=row2[:, :], in0=row2[:, :],
                                        scalar1=1.0 / 1024, scalar2=LN_EPS,
                                        op0=ALU.mult, op1=ALU.add).then_inc(s_ve.h, 1)
                    s_ve.v += 1
                    var_done = s_ve.v

                    @blk.scalar
                    def _(s, var=var_done):
                        s.wait_ge(s_ve.h, var)
                        s.activation(out=row3[:, :], in_=row2[:, :], func=AF.Ln)
                        s.activation(out=row2[:, :], in_=row3[:, :], func=AF.Exp,
                                     scale=-0.5).then_inc(s_sc.h, 1)
                    s_sc.v += 1
                    rstd_done = s_sc.v

                    @blk.vector
                    def _(v, rs=rstd_done):
                        v.wait_ge(s_sc.h, rs)
                        v.scalar_tensor_tensor(out=row3[:, :], in0=row1[:, :],
                                               scalar=-1.0 / 1024,
                                               in1=row2[:, :], op0=ALU.mult,
                                               op1=ALU.mult).then_inc(s_ve.h, 1)
                    s_ve.v += 1
                    rows_done = s_ve.v

                    @blk.tensor
                    def _(te, rd=rows_done):
                        te.wait_ge(s_ve.h, rd)
                        te.matmul(out=pD_[:, 0, :], lhsT=ones_row_f32[:, :],
                                  rhs=row2[:, :], start=True, stop=True)
                        te.matmul(out=pD_[:, 1, :], lhsT=ones_row_f32[:, :],
                                  rhs=row3[:, :], start=True, stop=True)\
                            .then_inc(s_te.h, 1)
                    s_te.v += 1
                    ab_done = s_te.v

                    @blk.vector
                    def _(v, slot=slot, b=b, ab=ab_done):
                        v.wait_ge(s_te.h, ab)
                        pAs[0].wait(v)
                        pAs[1].wait(v)
                        v.tensor_copy(out=b_sb[:, :], in_=pD_[:, 1, :])
                        for half in range(2):
                            for j in range(4):
                                dt = half * 4 + j
                                v.tensor_tensor(out=pA_[:, j % 2, :],
                                                in0=x_sb[:, slot, dt, :],
                                                in1=pD_[:, 0, :], op=ALU.mult)
                                i = v.tensor_tensor(out=xn_p2[:, b % 2, dt, :],
                                                    in0=pA_[:, j % 2, :],
                                                    in1=b_sb[:, :], op=ALU.add)
                        i.then_inc(s_ve.h, 1)
                    s_ve.v += 1
                    xn_done = s_ve.v
                    pRs.set(s_ve)
                    pDs[0].set(s_ve)
                    pDs[1].set(s_ve)
                    pAs[0].set(s_ve)
                    pAs[1].set(s_ve)

                    for which, wst, wsem2, wslot2, dram_w, dst in (
                            ("q", wq_strip, s_wq2, wq_slot, awq, sq_bf),
                            ("k", wk_strip, s_wk2, wk_slot, awk, sk_bf)):
                        for it in range(DT):
                            psl = it % 2
                            wsl = it % 2
                            wsem = wsem2[wsl]

                            @blk.sync
                            def _(q, b=b, it=it, wsl=wsl, wsem=wsem, wst=wst,
                                  dram_w=dram_w, wslot2=wslot2):
                                wslot2[wsl].wait(q)
                                q.dma_start(
                                    out=wst[:, wsl, :, :],
                                    in_=dram_w[:, it * 128:(it + 1) * 128]
                                    .rearrange("(dt p) f -> p dt f", p=128))\
                                    .then_inc(wsem.h, 16)
                            wsem.v += 16
                            w_ld = wsem.v

                            @blk.tensor
                            def _(te, b=b, it=it, psl=psl, wsl=wsl, xn=xn_done,
                                  wsem=wsem, wl=w_ld, wst=wst, which=which):
                                if it == 0 and which == "q":
                                    te.wait_ge(s_ve.h, xn)
                                te.wait_ge(wsem.h, wl)
                                pAs[psl].wait(te)
                                for dt in range(DT):
                                    i = te.matmul(
                                        out=pA_[:, psl, :],
                                        lhsT=wst[:, wsl, dt, :],
                                        rhs=xn_p2[:, b % 2, dt, :],
                                        start=(dt == 0), stop=(dt == DT - 1))
                                i.then_inc(s_te.h, 1)
                            s_te.v += 1
                            mm_done = s_te.v
                            wslot2[wsl].set(s_te)

                            @blk.scalar
                            def _(s, it=it, psl=psl, mm=mm_done, po_prev=s_po.v):
                                s.wait_ge(s_te.h, mm)
                                if po_prev:
                                    s.wait_ge(s_po.h, po_prev)
                                s.activation(out=t_rb[:, :], in_=pA_[:, psl, :],
                                             func=AF.Relu)
                                s.activation(out=a_sb[:, :], in_=pA_[:, psl, :],
                                             func=AF.Relu, scale=-1.0)
                                s.activation(out=t_eb[:, :], in_=a_sb[:, :],
                                             func=AF.Exp, scale=-1.0).then_inc(s_sc.h, 1)
                            s_sc.v += 1
                            elu_done = s_sc.v
                            pAs[psl].set(s_sc)

                            @blk.gpsimd
                            def _(g, b=b, it=it, el=elu_done, dst=dst, which=which):
                                g.wait_ge(s_sc.h, el)
                                if it == 0:
                                    (sqbf if which == "q" else skbf)[b % 2].wait(g)
                                g.tensor_tensor(out=dst[:, b % 2, it, :],
                                                in0=t_rb[:, :],
                                                in1=t_eb[:, :], op=ALU.add)\
                                    .then_inc(s_po.h, 1)
                            s_po.v += 1
                            trb_slot[it % 2].set(s_po)
                        if which == "q":
                            st2[(b, "sq")] = s_po.v
                        else:
                            st2[(b, "sk")] = s_po.v

                    for tt in range(NCH):
                        for eh in range(2):
                            psl = eh

                            @blk.tensor
                            def _(te, b=b, tt=tt, eh=eh, psl=psl, xn=xn_done,
                                  w=w_ready):
                                if tt == 0 and eh == 0:
                                    te.wait_ge(s_wld.h, w)
                                pBs[psl].wait(te)
                                for dt in range(DT):
                                    i = te.matmul(
                                        out=pB_[:, psl, :],
                                        lhsT=xn_p2[:, b % 2, dt, tt * 128:(tt + 1) * 128],
                                        rhs=wv_sb[:, dt, eh * 512:(eh + 1) * 512],
                                        start=(dt == 0), stop=(dt == DT - 1))
                                i.then_inc(s_te.h, 1)
                            s_te.v += 1
                            mm_done = s_te.v

                            @blk.vector
                            def _(v, b=b, tt=tt, eh=eh, psl=psl, mm=mm_done):
                                v.wait_ge(s_te.h, mm)
                                if tt == 0 and eh == 0:
                                    vbf[b % 2].wait(v)
                                v.tensor_copy(
                                    out=v_bf[:, b % 2, tt, eh * 512:(eh + 1) * 512],
                                    in_=pB_[:, psl, :]).then_inc(s_ve.h, 1)
                            s_ve.v += 1
                            pBs[psl].set(s_ve)
                    st2[(b, "v")] = s_ve.v

                def emit_tra(b):
                    sk_ready = st2[(b, "sk")]
                    sq_ready = st2[(b, "sq")]

                    for j in range(NCH):
                        psl = j % 2

                        @blk.tensor
                        def _(te, b=b, j=j, psl=psl, skr=sk_ready):
                            if j == 0:
                                te.wait_ge(s_po.h, skr)
                                sktok_slot.wait(te)
                            pCs[psl].wait(te)
                            for dt in range(DT):
                                i = te.transpose(
                                    out=pC_[:, psl, dt * 64:(dt + 1) * 64].bitcast(BF16),
                                    in_=sk_bf[:, b % 2, dt, j * 128:(j + 1) * 128],
                                    identity=idbf[:, :])
                            i.then_inc(s_te.h, 1)
                        s_te.v += 1
                        tr_done = s_te.v

                        @blk.vector
                        def _(v, j=j, psl=psl, td=tr_done):
                            v.wait_ge(s_te.h, td)
                            v.tensor_copy(out=sk_tok[:, j, :],
                                          in_=pC_[:, psl, :].bitcast(BF16))\
                                .then_inc(s_ve.h, 1)
                        s_ve.v += 1
                        pCs[psl].set(s_ve)
                    st2[(b, "sktok")] = s_ve.v

                    for j in range(NCH):
                        psl = j % 2
                        N = T - 128 * j

                        @blk.tensor
                        def _(te, b=b, j=j, psl=psl, N=N, sqr=sq_ready):
                            if j == 0:
                                te.wait_ge(s_po.h, sqr)
                            pAs[psl].wait(te)
                            for dt in range(DT):
                                i = te.matmul(
                                    out=pA_[:, psl, 0:N],
                                    lhsT=sk_bf[:, b % 2, dt, j * 128:(j + 1) * 128],
                                    rhs=sq_bf[:, b % 2, dt, j * 128:j * 128 + N],
                                    start=(dt == 0), stop=(dt == DT - 1))
                            i.then_inc(s_te.h, 1)
                        s_te.v += 1
                        mm_done = s_te.v

                        @blk.vector
                        def _(v, b=b, j=j, psl=psl, N=N, mm=mm_done):
                            v.wait_ge(s_te.h, mm)
                            if j == 0:
                                Abf[b % 2].wait(v)
                            v.tensor_tensor(out=A_bf[:, b % 2, AOFF[j]:AOFF[j] + N],
                                            in0=pA_[:, psl, 0:N],
                                            in1=mask_tpl[:, 0:N], op=ALU.mult)\
                                .then_inc(s_ve.h, 1)
                        s_ve.v += 1
                        pAs[psl].set(s_ve)
                    A_done = s_ve.v
                    st2[(b, "A")] = A_done

                    @blk.tensor
                    def _(te, b=b, Ad=A_done, zp=zupd_prev):
                        te.wait_ge(s_ve.h, Ad)
                        if b > 0:
                            te.wait_ge(s_ve.h, zp)
                        pDs[0].wait(te)
                        pDs[1].wait(te)
                        for j in range(NCH):
                            N = T - 128 * j
                            te.matmul(out=pD_[0:1, 0, j * 128:T], lhsT=ones_col_bf[:, :],
                                      rhs=A_bf[:, b % 2, AOFF[j]:AOFF[j] + N],
                                      start=(j == 0), stop=(b == 0 and j == NCH - 1))
                        if b > 0:
                            for dt in range(DT):
                                te.matmul(out=pD_[0:1, 0, :], lhsT=zc_bf[:, dt:dt + 1],
                                          rhs=sq_bf[:, b % 2, dt, :], start=False,
                                          stop=(dt == DT - 1))
                        for dt in range(DT):
                            i = te.matmul(out=pD_[0:1, 1, :], lhsT=mnc_bf[:, dt:dt + 1],
                                          rhs=sq_bf[:, b % 2, dt, :], start=(dt == 0),
                                          stop=(dt == DT - 1))
                        i.then_inc(s_te.h, 1)
                    s_te.v += 1
                    dn_done = s_te.v

                    @blk.vector
                    def _(v, dn=dn_done):
                        v.wait_ge(s_te.h, dn)
                        v.tensor_scalar(out=row1[:, :], in0=pD_[0:1, 0, :], scalar1=EPS,
                                        scalar2=None, op0=ALU.max)
                        v.tensor_scalar(out=row2[:, :], in0=pD_[0:1, 1, :], scalar1=EPS,
                                        scalar2=None, op0=ALU.max).then_inc(s_ve.h, 1)
                    s_ve.v += 1
                    clip_done = s_ve.v

                    @blk.scalar
                    def _(s, cd=clip_done):
                        s.wait_ge(s_ve.h, cd)
                        s.activation(out=row1[:, :], in_=row1[:, :], func=AF.Ln)
                        s.activation(out=row1[:, :], in_=row1[:, :], func=AF.Exp,
                                     scale=-1.0, bias=ln1mg)
                        s.activation(out=row2[:, :], in_=row2[:, :], func=AF.Ln)
                        s.activation(out=row2[:, :], in_=row2[:, :], func=AF.Exp,
                                     scale=-1.0, bias=lnga).then_inc(s_sc.h, 1)
                    s_sc.v += 1
                    rr_done = s_sc.v

                    @blk.tensor
                    def _(te, rr=rr_done):
                        te.wait_ge(s_sc.h, rr)
                        pDs[0].wait(te)
                        pDs[1].wait(te)
                        te.matmul(out=pD_[:, 0, :], lhsT=ones_row_f32[:, :],
                                  rhs=row1[:, :], start=True, stop=True)
                        te.matmul(out=pD_[:, 1, :], lhsT=ones_row_f32[:, :],
                                  rhs=row2[:, :], start=True, stop=True)\
                            .then_inc(s_te.h, 1)
                    s_te.v += 1
                    rrep_done = s_te.v
                    pRs.set(s_te)

                    @blk.vector
                    def _(v, b=b, rp=rrep_done):
                        v.wait_ge(s_te.h, rp)
                        v.tensor_copy(out=rl_bc[:, b % 2, :], in_=pD_[:, 0, :])
                        v.tensor_copy(out=rm_bc[:, b % 2, :], in_=pD_[:, 1, :])\
                            .then_inc(s_ve.h, 1)
                    s_ve.v += 1
                    st2[(b, "rbc")] = s_ve.v
                    pDs[0].set(s_ve)
                    pDs[1].set(s_ve)

                def emit_zupd(b):
                    nonlocal zupd_prev

                    @blk.vector
                    def _(v, b=b, skr=st2[(b, "sk")], dnd=st2[(b, "dn", )
                          if False else (b, "A")]):
                        v.wait_ge(s_po.h, skr)
                        v.reduce_sum(out=zred[:, :], in_=sk_bf[:, b % 2, :, :],
                                     axis=AX.X)
                        if b == 0:
                            v.tensor_copy(out=zc[:, :], in_=zred[:, :])
                        else:
                            v.tensor_tensor(out=zc[:, :], in0=zc[:, :],
                                            in1=zred[:, :], op=ALU.add)
                        v.tensor_copy(out=zc_bf[:, :], in_=zc[:, :])\
                            .then_inc(s_ve.h, 1)
                    s_ve.v += 1
                    zupd_prev = s_ve.v

                def emit_pb(b):
                    nonlocal supd_prev
                    v_done = st2[(b, "v")]
                    rbc_done = st2[(b, "rbc")]
                    for et in range(DT):
                        psl = et % 2
                        ssl = et % 2
                        msl = et % 2
                        msem = s_wm2[msl]

                        @blk.scalar
                        def _(s, et=et, msl=msl, msem=msem):
                            wm_slot[msl].wait(s)
                            s.dma_start(
                                out=mem_strip[:, msl, :, :],
                                in_=awm[:, et * 128:(et + 1) * 128]
                                .rearrange("(dt p) f -> p dt f", p=128))\
                                .then_inc(msem.h, 16)
                        msem.v += 16
                        mem_ld = msem.v

                        if b > 0:
                            @blk.scalar
                            def _(s, et=et, ssl=ssl, sup=supd_prev):
                                Sbf_slot[ssl].wait(s)
                                if et == 0:
                                    s.wait_ge(s_se.h, sup)
                                s.copy(out=S_bfs[:, ssl, :, :],
                                       in_=S_f[:, :, et * 128:(et + 1) * 128])\
                                    .then_inc(s_sc.h, 1)
                            s_sc.v += 1
                            strip_done = s_sc.v
                        else:
                            strip_done = None

                        @blk.tensor
                        def _(te, b=b, et=et, psl=psl, ssl=ssl,
                              vd=v_done, sd=strip_done, msem=msem, ml=mem_ld):
                            if et == 0:
                                te.wait_ge(s_ve.h, vd)
                            te.wait_ge(msem.h, ml)
                            pAs[psl].wait(te)
                            pBs[psl].wait(te)
                            first = True
                            if b > 0:
                                te.wait_ge(s_sc.h, sd)
                                for dt in range(DT):
                                    te.matmul(out=pA_[:, psl, :],
                                              lhsT=S_bfs[:, ssl, dt, :],
                                              rhs=sq_bf[:, b % 2, dt, :],
                                              start=(dt == 0), stop=False)
                                first = False
                            for j in range(NCH):
                                N = T - 128 * j
                                te.matmul(out=pA_[:, psl, j * 128:T],
                                          lhsT=v_bf[:, b % 2, j, et * 128:(et + 1) * 128],
                                          rhs=A_bf[:, b % 2, AOFF[j]:AOFF[j] + N],
                                          start=(first and j == 0), stop=(j == NCH - 1))
                            for dt in range(DT):
                                i = te.matmul(out=pB_[:, psl, :],
                                              lhsT=mem_strip[:, et % 2, dt, :],
                                              rhs=sq_bf[:, b % 2, dt, :],
                                              start=(dt == 0), stop=(dt == DT - 1))
                            i.then_inc(s_te.h, 1)
                        s_te.v += 1
                        nm_done = s_te.v
                        if b > 0:
                            Sbf_slot[ssl].set(s_te)
                        wm_slot[et % 2].set(s_te)

                        @blk.vector
                        def _(v, b=b, et=et, psl=psl, nm=nm_done, rb=rbc_done):
                            v.wait_ge(s_te.h, nm)
                            if et == 0:
                                v.wait_ge(s_ve.h, rb)
                            tvslot[et % 2].wait(v)
                            v.tensor_tensor(out=tv1[:, et % 2, :], in0=pB_[:, psl, :],
                                            in1=rm_bc[:, b % 2, :], op=ALU.mult)
                            v.tensor_tensor(out=tv2[:, et % 2, :], in0=pA_[:, psl, :],
                                            in1=rl_bc[:, b % 2, :], op=ALU.mult)\
                                .then_inc(s_ve.h, 1)
                        s_ve.v += 1
                        tv_done = s_ve.v
                        pAs[psl].set(s_ve)
                        pBs[psl].set(s_ve)

                        @blk.gpsimd
                        def _(g, et=et, tv=tv_done):
                            g.wait_ge(s_ve.h, tv)
                            if et == 0:
                                O_slot.wait(g)
                            g.tensor_tensor(out=O_bf[:, et, :], in0=tv1[:, et % 2, :],
                                            in1=tv2[:, et % 2, :], op=ALU.add)\
                                .then_inc(s_po.h, 1)
                        s_po.v += 1
                        tvslot[et % 2].set(s_po)
                    st2[(b, "O")] = s_po.v

                def emit_sup_mm(b):
                    if b >= NBLK - 1:
                        return
                    sktok_done = st2[(b, "sktok")]
                    v_done = st2[(b, "v")]
                    mms = []
                    base = s_se.v
                    for k in range(16):
                        dt, eh = k // 2, k % 2
                        psl = k % 2

                        @blk.tensor
                        def _(te, b=b, k=k, dt=dt, eh=eh, psl=psl, skd=sktok_done,
                              vd=v_done, base=base):
                            if k == 0:
                                te.wait_ge(s_ve.h, skd)
                                te.wait_ge(s_ve.h, vd)
                                pCs[0].wait(te)
                                pCs[1].wait(te)
                            if k >= 2:
                                te.wait_ge(s_se.h, base + k - 1)
                            for j in range(NCH):
                                i = te.matmul(
                                    out=pC_[:, psl, :],
                                    lhsT=sk_tok[:, j, dt * 128:(dt + 1) * 128],
                                    rhs=v_bf[:, b % 2, j, eh * 512:(eh + 1) * 512],
                                    start=(j == 0), stop=(j == NCH - 1))
                            i.then_inc(s_te.h, 1)
                        s_te.v += 1
                        mms.append((dt, eh, psl, s_te.v))
                    st2[(b, "supmm")] = mms

                def emit_sup_evac(b):
                    nonlocal supd_prev
                    if b >= NBLK - 1:
                        return
                    for dt, eh, psl, mm_done in st2[(b, "supmm")]:
                        @blk.vector
                        def _(v, b=b, dt=dt, eh=eh, psl=psl, mm=mm_done):
                            v.wait_ge(s_te.h, mm)
                            dst = S_f[:, dt, eh * 512:(eh + 1) * 512]
                            if b == 0:
                                i = v.tensor_copy(out=dst, in_=pC_[:, psl, :])
                            else:
                                i = v.tensor_tensor(out=dst, in0=dst,
                                                    in1=pC_[:, psl, :], op=ALU.add)
                            i.then_inc(s_se.h, 1)
                        s_se.v += 1
                        pCs[psl].set(s_se)
                    supd_prev = s_se.v
                    st2[(b, "sup")] = s_se.v

                def emit_wo(b):
                    slot = b % 2
                    O_done = st2[(b, "O")]
                    for ft in range(DT):
                        psl = ft % 2
                        wsl = ft % 2
                        swo = s_wo2[wsl]

                        @blk.scalar
                        def _(s, b=b, ft=ft, wsl=wsl, swo=swo):
                            wo_slot[wsl].wait(s)
                            s.dma_start(
                                out=wo_strip[:, wsl, :, :],
                                in_=awo[:, ft * 128:(ft + 1) * 128]
                                .rearrange("(et p) f -> p et f", p=128))\
                                .then_inc(swo.h, 16)
                        swo.v += 16
                        wo_ld = swo.v

                        @blk.tensor
                        def _(te, ft=ft, psl=psl, wsl=wsl, od=O_done, swo=swo,
                              wl=wo_ld):
                            if ft == 0:
                                te.wait_ge(s_po.h, od)
                            te.wait_ge(swo.h, wl)
                            pDs[psl].wait(te)
                            for et in range(DT):
                                i = te.matmul(out=pD_[:, psl, :],
                                              lhsT=wo_strip[:, wsl, et, :],
                                              rhs=O_bf[:, et, :],
                                              start=(et == 0), stop=(et == DT - 1))
                            i.then_inc(s_te.h, 1)
                        s_te.v += 1
                        mm_done = s_te.v
                        wo_slot[wsl].set(s_te)
                        if ft == DT - 1:
                            O_slot.set(s_te)

                        @blk.vector
                        def _(v, slot=slot, ft=ft, psl=psl, mm=mm_done):
                            v.wait_ge(s_te.h, mm)
                            dst = x_sb[:, slot, ft, :]
                            v.tensor_tensor(out=dst, in0=dst, in1=pD_[:, psl, :],
                                            op=ALU.add).then_inc(s_ve.h, 1)
                        s_ve.v += 1
                        pDs[psl].set(s_ve)
                    x2_done = s_ve.v
                    s_st = s_st2[b % 2]

                    @blk.sync
                    def _(q, b=b, x2=x2_done, s_st=s_st):
                        q.wait_ge(s_ve.h, x2)
                        q.dma_start(
                            out=x2f[:, :, b * T:(b + 1) * T].rearrange("dt p t -> p dt t"),
                            in_=x_sb[:, b % 2, :, :]).then_inc(s_st.h, 16)
                    s_st.v += 16
                    xslot[slot].set(s_st)

                    sqbf[b % 2].set(s_te)
                    skbf[b % 2].set(s_te)
                    vbf[b % 2].set(s_te)
                    Abf[b % 2].set(s_te)
                    sktok_slot.set(s_te)

                emit_ldx(0)
                emit_ldx(1)
                for b in range(0, NBLK, 2):
                    emit_lnq(b)
                    emit_lnq(b + 1)
                    emit_tra(b)
                    emit_zupd(b)
                    emit_tra(b + 1)
                    emit_zupd(b + 1)
                    emit_pb(b)
                    emit_sup_mm(b)
                    emit_sup_evac(b)
                    emit_wo(b)
                    if b + 2 < NBLK:
                        emit_ldx(b + 2)
                    emit_pb(b + 1)
                    emit_sup_mm(b + 1)
                    emit_sup_evac(b + 1)
                    emit_wo(b + 1)
                    if b + 3 < NBLK:
                        emit_ldx(b + 3)

                @blk.sync
                def _(q, vals=[(s.h, s.v) for s in s_st2]):
                    for h, v_ in vals:
                        q.wait_ge(h, v_)

        with ExitStack() as ps, nc.named_scope("P3"):
            Ep = ps.enter_context
            gw1_sb = Ep(nc.sbuf_tensor("gw1_sb3", [128, DT, 512], BF16))
            gw2_sb = Ep(nc.sbuf_tensor("gw2_sb3", [128, 4, 256], BF16))
            gw3_sb = Ep(nc.sbuf_tensor("gw3_sb3", [128, 2, 1], BF16))
            wstage = Ep(nc.sbuf_tensor("wstage3", [128, 2, D], F32))
            macc = Ep(nc.sbuf_tensor("macc", [128, 8], F32))
            mred = Ep(nc.sbuf_tensor("mred", [128, 8], F32))
            mrow_bf3 = Ep(nc.sbuf_tensor("mrow_bf3", [1, 1], BF16))
            pC3 = Ep(nc.psum_tensor("pC3", [128, 2, T], F32))
            pAB3 = Ep(nc.psum_tensor("pAB3", [128, 2, T], F32))
            pR3 = Ep(nc.psum_tensor("pR3", [1, 2 * T], F32))
            pG3 = Ep(nc.psum_tensor("pG3", [128, 2, 16], F32))

            s_ld2 = [mksem("g_ld0"), mksem("g_ld1")]
            s_te = mksem("g_te")
            s_ve = mksem("g_ve")
            s_sc = mksem("g_sc")
            s_wld2 = [mksem("g_wld0"), mksem("g_wld1")]
            s_wcast = mksem("g_wcast")

            with DBlk() as blk:
                emit_weight_load(blk, gw1, gw1_sb, DT, 512, wstage, s_wld2, s_wcast)
                emit_weight_load(blk, gw2, gw2_sb, 4, 256, wstage, s_wld2, s_wcast)
                emit_weight_load(blk, gw3, gw3_sb, 2, 1, wstage, s_wld2, s_wcast)
                gw_ready = s_wcast.v

                xslot = [Slot(), Slot()]
                xslot_sc = [Slot(), Slot()]
                xnslot = Slot()
                pRs = Slot()
                pABs = Slot()
                pCs = Slot()

                for b in range(NBLK):
                    slot = b % 2
                    s_ld = s_ld2[slot]

                    @blk.sync
                    def _(q, b=b, slot=slot, s_ld=s_ld):
                        xslot[slot].wait(q)
                        xslot_sc[slot].wait(q)
                        q.dma_start(
                            out=x_sb[:, slot, :, :],
                            in_=x2f[:, :, b * T:(b + 1) * T].rearrange("dt p t -> p dt t"),
                        ).then_inc(s_ld.h, 16)
                    s_ld.v += 16
                    ld_done = s_ld.v

                    @GS(blk)
                    def _(s, slot=slot, s_ld=s_ld, ld=ld_done):
                        s.wait_ge(s_ld.h, ld)
                        xnslot.wait(s)
                        s.activation(out=xsq_bf[:, :, :], in_=x_sb[:, slot, :, :],
                                     func=AF.Square).then_inc(s_sc.h, 1)
                    s_sc.v += 1
                    sq_done = s_sc.v
                    xslot_sc[slot].set(s_sc)

                    @GV(blk)
                    def _(v, slot=slot, s_ld=s_ld, ld=ld_done):
                        v.wait_ge(s_ld.h, ld)
                        xnslot.wait(v)
                        v.tensor_copy(out=xb_bf[:, :, :], in_=x_sb[:, slot, :, :])\
                            .then_inc(s_ve.h, 1)
                    s_ve.v += 1
                    xb_done = s_ve.v

                    @blk.tensor
                    def _(te, sq=sq_done, xb=xb_done):
                        te.wait_ge(s_sc.h, sq)
                        te.wait_ge(s_ve.h, xb)
                        pRs.wait(te)
                        for dt in range(DT):
                            te.matmul(out=pR3[0:1, 0:T], lhsT=ones_col_bf[:, :],
                                      rhs=xb_bf[:, dt, :], start=(dt == 0),
                                      stop=(dt == DT - 1))
                        for dt in range(DT):
                            i = te.matmul(out=pR3[0:1, T:2 * T], lhsT=ones_col_bf[:, :],
                                          rhs=xsq_bf[:, dt, :], start=(dt == 0),
                                          stop=(dt == DT - 1))
                        i.then_inc(s_te.h, 1)
                    s_te.v += 1
                    sums_done = s_te.v
                    xnslot.set(s_te)

                    srow = pR3[0:1, 0:T]
                    qrow = pR3[0:1, T:2 * T]

                    @GV(blk)
                    def _(v, sd=sums_done):
                        v.wait_ge(s_te.h, sd)
                        v.tensor_copy(out=row1[:, :], in_=srow)
                        v.scalar_tensor_tensor(out=row2[:, :], in0=row1[:, :],
                                               scalar=1.0 / 1024,
                                               in1=row1[:, :], op0=ALU.mult, op1=ALU.mult)
                        v.tensor_tensor(out=row2[:, :], in0=qrow, in1=row2[:, :],
                                        op=ALU.subtract)
                        v.tensor_scalar(out=row2[:, :], in0=row2[:, :], scalar1=1.0 / 1024,
                                        scalar2=LN_EPS, op0=ALU.mult, op1=ALU.add)\
                            .then_inc(s_ve.h, 1)
                    s_ve.v += 1
                    var_done = s_ve.v

                    @GS(blk)
                    def _(s, var=var_done):
                        s.wait_ge(s_ve.h, var)
                        s.activation(out=row3[:, :], in_=row2[:, :], func=AF.Ln)
                        s.activation(out=row2[:, :], in_=row3[:, :], func=AF.Exp,
                                     scale=-0.5).then_inc(s_sc.h, 1)
                    s_sc.v += 1
                    rstd_done = s_sc.v

                    @GV(blk)
                    def _(v, rs=rstd_done):
                        v.wait_ge(s_sc.h, rs)
                        v.scalar_tensor_tensor(out=row3[:, :], in0=row1[:, :],
                                               scalar=-1.0 / 1024,
                                               in1=row2[:, :], op0=ALU.mult, op1=ALU.mult)\
                            .then_inc(s_ve.h, 1)
                    s_ve.v += 1
                    rows_done = s_ve.v
                    pRs.set(s_ve)

                    @blk.tensor
                    def _(te, rd=rows_done):
                        te.wait_ge(s_ve.h, rd)
                        pABs.wait(te)
                        te.matmul(out=pAB3[:, 0, :], lhsT=ones_row_f32[:, :],
                                  rhs=row2[:, :], start=True, stop=True)
                        te.matmul(out=pAB3[:, 1, :], lhsT=ones_row_f32[:, :],
                                  rhs=row3[:, :], start=True, stop=True)\
                            .then_inc(s_te.h, 1)
                    s_te.v += 1
                    ab_done = s_te.v

                    @GV(blk)
                    def _(v, slot=slot, b=b, ab=ab_done):
                        v.wait_ge(s_te.h, ab)
                        pCs.wait(v)
                        for dt in range(DT):
                            v.tensor_tensor(out=pC3[:, dt % 2, :],
                                            in0=x_sb[:, slot, dt, :],
                                            in1=pAB3[:, 0, :], op=ALU.mult)
                            v.reduce_sum(out=mred[:, dt:dt + 1], in_=pC3[:, dt % 2, :],
                                         axis=AX.X)
                        v.reduce_sum(out=row1[0:1, 0:1], in_=pAB3[0:1, 1, :], axis=AX.X)
                        if b == 0:
                            v.tensor_copy(out=macc[:, :], in_=mred[:, :])
                            i = v.tensor_copy(out=row2[0:1, 0:1], in_=row1[0:1, 0:1])
                        else:
                            v.tensor_tensor(out=macc[:, :], in0=macc[:, :],
                                            in1=mred[:, :], op=ALU.add)
                            i = v.tensor_tensor(out=row2[0:1, 0:1], in0=row2[0:1, 0:1],
                                                in1=row1[0:1, 0:1], op=ALU.add)
                        i.then_inc(s_ve.h, 1)
                    s_ve.v += 1
                    xslot[slot].set(s_ve)
                    pABs.set(s_ve)
                    pCs.set(s_ve)

                @GS(blk)
                def _(s, vv=s_ve.v):
                    s.wait_ge(s_ve.h, vv)
                    s.copy(out=mrow_bf3[0:1, 0:1], in_=row2[0:1, 0:1]).then_inc(s_sc.h, 1)
                s_sc.v += 1
                bsc = s_sc.v

                @blk.tensor
                def _(te, bs=bsc):
                    te.wait_ge(s_sc.h, bs)
                    te.matmul(out=pG3[:, 0, 0:1], lhsT=ones_row_bf[:, :],
                              rhs=mrow_bf3[0:1, 0:1], start=True, stop=True,
                              skip_group_check=True).then_inc(s_te.h, 1)
                s_te.v += 1
                bsb = s_te.v

                @GV(blk)
                def _(v, wc=gw_ready, bs=bsb):
                    v.wait_ge(s_wcast.h, wc)
                    v.wait_ge(s_te.h, bs)
                    v.tensor_scalar(out=macc[:, :], in0=macc[:, :],
                                    scalar1=pG3[:, 0, 0:1], scalar2=1.0 / SEQ,
                                    op0=ALU.add, op1=ALU.mult)
                    v.tensor_copy(out=m_bf[:, :], in_=macc[:, :]).then_inc(s_ve.h, 1)
                s_ve.v += 1

                emit_gfnet(blk, gw1_sb, gw2_sb, gw3_sb, pG3, c0_2, c1_2,
                           (s_te, s_ve, s_sc), m_ready=(s_ve, s_ve.v))

        emit_ffn_pass("p4", x2f, None, c0_2, c1_2, True)

        if dump is not None:
            src = {"x0f": x0f, "x1f": x1f, "x2f": x2f}[dump]
            with nc.Block() as blk, nc.semaphore("dbg_sem") as dsem:
                @blk.sync
                def _(q):
                    q.dma_start(out=dbg[:, :, :], in_=src[:, :, :]).then_inc(dsem, 16)
                    q.wait_ge(dsem, 16)


    return nc, dbg


def _prep_inputs(kwargs):
    x = np.ascontiguousarray(kwargs["L0_attn_hidden_states"], dtype=np.float32)
    shared = {
        "wqT": np.ascontiguousarray(kwargs["L0_attn_w_q"].T),
        "wkT": np.ascontiguousarray(kwargs["L0_attn_w_k"].T),
        "wvT": np.ascontiguousarray(kwargs["L0_attn_w_v"].T),
        "woT": np.ascontiguousarray(kwargs["L0_attn_w_o"].T),
        "memw": np.ascontiguousarray(kwargs["L0_attn_memory"]),
        "mn": np.ascontiguousarray(kwargs["L0_attn_memory_norm"]),
        "gate": np.ascontiguousarray(kwargs["L0_attn_gate"]),
        "cw1": np.ascontiguousarray(kwargs["L0_ffn_cn_w1"]),
        "cb1": np.ascontiguousarray(kwargs["L0_ffn_cn_b1"]),
        "cw2": np.ascontiguousarray(kwargs["L0_ffn_cn_w2"]),
        "cb2": np.ascontiguousarray(kwargs["L0_ffn_cn_b2"]),
        "cw3": np.ascontiguousarray(kwargs["L0_ffn_cn_w3"]),
        "cb3": np.ascontiguousarray(kwargs["L0_ffn_cn_b3"]),
        "cw4": np.ascontiguousarray(kwargs["L0_ffn_cn_w4"]),
        "cb4": np.ascontiguousarray(kwargs["L0_ffn_cn_b4"]),
        "gw1": np.ascontiguousarray(kwargs["L0_ffn_gc_w1"]),
        "gb1": np.ascontiguousarray(kwargs["L0_ffn_gc_b1"]),
        "gw2": np.ascontiguousarray(kwargs["L0_ffn_gc_w2"]),
        "gb2": np.ascontiguousarray(kwargs["L0_ffn_gc_b2"]),
        "gw3": np.ascontiguousarray(kwargs["L0_ffn_gc_w3"]),
        "gb3": np.ascontiguousarray(kwargs["L0_ffn_gc_b3"]),
        "ar": np.ascontiguousarray(kwargs["L0_ffn_adaptive_range"]),
    }
    in_maps = []
    for b in range(8):
        m = {"hs": np.ascontiguousarray(x[b])}
        m.update(shared)
        in_maps.append(m)
    return in_maps


def kernel(**kwargs):
    in_maps = _prep_inputs(kwargs)
    nc, _ = build_program()
    res = run_bass_kernel_spmd(nc, in_maps, core_ids=list(range(8)))
    return np.stack([res.results[i]["out"] for i in range(8)], axis=0)
